# version 01 (final)
# baseline (speedup 1.0000x reference)
import numpy as np
import concourse.bass as bass
import concourse.mybir as mybir

F32 = mybir.dt.float32
BF16 = mybir.dt.bfloat16
I32 = mybir.dt.int32
AF = mybir.ActivationFunctionType
OP = mybir.AluOpType

TQ = 4096
TO = 2048
CH = 1024
DM = 1024
S = 16
G = 8
DH = 128
H = 8
EPS_RMS = 1.1920929e-07
EPS_LN = 1e-05

DEBUG_OUTS = []


class Ctx:

    def __init__(self, nc):
        self.nc = nc
        self.sem = {}
        self.cnt = {}
        for name in ("qS", "qG", "qA", "eT", "eV", "eS", "eG", "CC", "qO"):
            self.sem[name] = nc.alloc_semaphore(name)
            self.cnt[name] = 0

    def inc(self, name, amt=1):
        self.cnt[name] += amt
        return self.sem[name], self.cnt[name]

    def v(self, name):
        return self.cnt[name]

    def s(self, name):
        return self.sem[name]


def split_hi_lo(v, hi, lo, src32, stage=None):
    v.tensor_copy(hi, src32)
    v.tensor_tensor(lo, src32, hi, op=OP.subtract)


def mm3(t, ps, lhs_hi, lhs_lo, rhs_hi, rhs_lo, start, stop):
    t.matmul(ps, lhs_hi, rhs_hi, start=start, stop=False)
    t.matmul(ps, lhs_hi, rhs_lo, start=False, stop=False)
    return t.matmul(ps, lhs_lo, rhs_hi, start=False, stop=stop)


def host_layout(inputs, core):
    b, j = core // 2, core % 2
    f32 = np.float32
    g = lambda k: np.ascontiguousarray(np.asarray(inputs[k], dtype=f32))

    x = g("L0_attn_x")[b]
    chs = slice(j * CH, (j + 1) * CH)

    def chvec(vec):
        return np.ascontiguousarray(vec[chs].reshape(G, 128).T)

    inp = {}
    inp["x_full"] = x
    inp["x_own"] = np.ascontiguousarray(
        np.concatenate([x[(2 * i + j) * 512:(2 * i + j + 1) * 512] for i in range(4)], axis=0))
    w_in = g("L0_attn_in_proj_w")
    wt = np.concatenate([w_in[chs].T, w_in[2048 + j * CH:2048 + (j + 1) * CH].T], axis=1)
    inp["w_in_t"] = np.ascontiguousarray(wt)
    cw = g("L0_attn_conv_w")[chs]
    inp["conv_w"] = np.ascontiguousarray(cw.reshape(G, 128, 4).transpose(1, 0, 2).reshape(128, G * 4))
    inp["conv_b"] = chvec(g("L0_attn_conv_b"))
    inp["w_xp_t"] = np.ascontiguousarray(g("L0_attn_x_proj_w")[:, chs].T)
    inp["w_dt_t"] = np.ascontiguousarray(g("L0_attn_dt_proj_w")[chs].T)
    inp["dt_b"] = chvec(g("L0_attn_dt_proj_b"))
    al = g("L0_attn_A_log")[chs]
    inp["A_log"] = np.ascontiguousarray(al.reshape(G, 128, S).transpose(1, 0, 2).reshape(128, G * S))
    inp["Dvec"] = chvec(g("L0_attn_D"))
    inp["w_out_t"] = np.ascontiguousarray(g("L0_attn_out_proj_w")[:, chs].T)
    inp["w_m1_t"] = np.ascontiguousarray(g("L0_attn_mlp_w1").T)
    inp["w_m2_t"] = np.ascontiguousarray(g("L0_attn_mlp_w2").T)
    inp["w_q_t"] = np.ascontiguousarray(g("L0_ffn_q_w").T)
    inp["w_k_t"] = np.ascontiguousarray(g("L0_ffn_k_w").T)
    inp["w_v_t"] = np.ascontiguousarray(g("L0_ffn_v_w").T)
    inp["w_o_t"] = np.ascontiguousarray(g("L0_ffn_o_w").T)

    def fvec(vec):
        return np.ascontiguousarray(np.asarray(vec, f32).reshape(G, 128).T)

    inp["b_q"] = fvec(g("L0_ffn_q_b"))
    inp["b_k"] = fvec(g("L0_ffn_k_b"))
    inp["b_v_row"] = np.asarray(g("L0_ffn_v_b"), f32).reshape(1, DM)
    inp["b_o"] = fvec(g("L0_ffn_o_b"))
    inp["norm_w"] = fvec(g("L0_ffn_norm_w"))
    inp["norm_b"] = fvec(g("L0_ffn_norm_b"))
    inp["ident"] = np.eye(128, dtype=f32)
    inp["identb"] = np.eye(128, dtype=f32)
    inp["meta"] = np.array([[j, 1 - j, 0, 0]], np.int32)
    mfv = np.zeros((128, 2), f32); mfv[:, 0] = 1.0 - j; mfv[:, 1] = j
    inp["meta_f"] = mfv
    return inp


class Emit:

    def __init__(self, eng):
        self.e = eng
        self.last = None

    def __getattr__(self, name):
        fn = getattr(self.e, name)
        if not callable(fn):
            return fn

        def wrap(*a, **kw):
            r = fn(*a, **kw)
            if hasattr(r, "then_inc"):
                self.last = r
            return r
        return wrap


def declare_io(nc):
    io = {}
    specs = {
        "x_full": ([TQ, DM], F32), "x_own": ([TO, DM], F32),
        "w_in_t": ([DM, 2 * CH], F32), "conv_w": ([128, G * 4], F32), "conv_b": ([128, G], F32),
        "w_xp_t": ([CH, 96], F32), "w_dt_t": ([64, CH], F32), "dt_b": ([128, G], F32),
        "A_log": ([128, G * S], F32), "Dvec": ([128, G], F32),
        "w_out_t": ([CH, DM], F32), "w_m1_t": ([DM, 4 * DM], F32), "w_m2_t": ([4 * DM, DM], F32),
        "w_q_t": ([DM, DM], F32), "w_k_t": ([DM, DM], F32), "w_v_t": ([DM, DM], F32), "w_o_t": ([DM, DM], F32),
        "b_q": ([128, G], F32), "b_k": ([128, G], F32), "b_v_row": ([1, DM], F32), "b_o": ([128, G], F32),
        "norm_w": ([128, G], F32), "norm_b": ([128, G], F32),
        "ident": ([128, 128], F32), "identb": ([128, 128], F32), "meta": ([1, 4], I32),
        "meta_f": ([128, 2], F32),
    }
    for k, (shape, dt) in specs.items():
        io[k] = nc.dram_tensor(k, shape, dt, kind="ExternalInput").ap()
    io["y_out"] = nc.dram_tensor("y_out", [TO, DM], F32, kind="ExternalOutput").ap()
    io["sp_sz"] = nc.dram_tensor("sp_sz", [CH, TQ], F32).ap()
    io["sp_xc_hi"] = nc.dram_tensor("sp_xc_hi", [CH, TQ], BF16).ap()
    io["sp_xc_lo"] = nc.dram_tensor("sp_xc_lo", [CH, TQ], BF16).ap()
    io["sp_bc"] = nc.dram_tensor("sp_bc", [32, TQ], F32).ap()
    io["ar_in"] = nc.dram_tensor("ar_in", [96, TQ], F32).ap()
    io["ar_out"] = nc.dram_tensor("ar_out", [96, TQ], F32).ap()
    io["bar_in"] = nc.dram_tensor("bar_in", [1, 32], F32).ap()
    io["bar_out"] = nc.dram_tensor("bar_out", [1, 32], F32).ap()
    io["sp_q_hi"] = nc.dram_tensor("sp_q_hi", [DM, TO], BF16).ap()
    io["sp_q_lo"] = nc.dram_tensor("sp_q_lo", [DM, TO], BF16).ap()
    io["sh_mm"] = nc.dram_tensor("sh_mm", [2, 2, 4, DM, 512], F32, addr_space="Shared").ap()
    io["sp_mmp"] = nc.dram_tensor("sp_mmp", [8, DM, 512], F32).ap()
    io["sp_ksth"] = nc.dram_tensor("sp_ksth", [DM, TO], BF16).ap()
    io["sp_kstl"] = nc.dram_tensor("sp_kstl", [DM, TO], BF16).ap()
    io["sp_vst"] = nc.dram_tensor("sp_vst", [TO, DM], F32).ap()
    io["sh_k_hi"] = nc.dram_tensor("sh_k_hi", [2, DM, TO], BF16, addr_space="Shared").ap()
    io["sh_k_lo"] = nc.dram_tensor("sh_k_lo", [2, DM, TO], BF16, addr_space="Shared").ap()
    io["sh_vt"] = nc.dram_tensor("sh_vt", [2, TO, DM], F32, addr_space="Shared").ap()
    io["sh_st"] = nc.dram_tensor("sh_st", [2, 1, 64], F32, addr_space="Shared").ap()
    return io


def pair_barrier(ctx, g):
    nc = ctx.nc
    g.collective_compute(
        "AllReduce", OP.add, replica_groups=[[0, 1], [2, 3], [4, 5], [6, 7]],
        ins=[ctx.io["bar_in"].opt()], outs=[ctx.io["bar_out"].opt()],
    ).then_inc(*_one(ctx, "CC"))
    g.wait_ge(ctx.s("CC"), ctx.v("CC"))


def _one(ctx, name):
    sem, val = ctx.inc(name)
    return sem, 1


def _dma(ctx, qname):
    sem, val = ctx.inc(qname, 16)
    return sem, 16


def emit(nc, engname, fn):
    with nc.Block() as blk:
        getattr(blk, engname)(fn)


def build(debug=None):
    debug = debug or set()
    nc = bass.Bass("TRN2", target_bir_lowering=False)
    io = declare_io(nc)
    ctx = Ctx(nc)
    ctx.io = io
    dbg = {}
    if "mamba1" in debug:
        dbg["xdbl"] = nc.dram_tensor("dbg_xdbl", [96, TQ], F32, kind="ExternalOutput").ap()
        dbg["sz"] = nc.dram_tensor("dbg_sz", [CH, TQ], F32, kind="ExternalOutput").ap()
        dbg["xc_hi"] = nc.dram_tensor("dbg_xc_hi", [CH, TQ], BF16, kind="ExternalOutput").ap()
        dbg["xc_lo"] = nc.dram_tensor("dbg_xc_lo", [CH, TQ], BF16, kind="ExternalOutput").ap()

    from contextlib import ExitStack
    outer = ExitStack()

    sb = lambda name, shape, dt: outer.enter_context(nc.sbuf_tensor(name, shape, dt))
    ident = sb("identc", [128, 128], F32)
    identb = sb("identbc", [128, 128], BF16)
    consts = {}
    for k, shape in [("conv_w", [128, G * 4]), ("conv_b", [128, G]), ("dt_b", [128, G]),
                     ("A_log", [128, G * S]), ("Dvec", [128, G]), ("b_q", [128, G]),
                     ("b_k", [128, G]), ("b_o", [128, G]), ("norm_w", [128, G]),
                     ("norm_b", [128, G]), ("meta", [1, 4])]:
        consts[k] = sb("c_" + k, shape, I32 if k == "meta" else F32)
    Amat = sb("c_A", [128, G * S], F32)
    cb = sb("c_bias", [128, 8], F32)
    mneg_o = sb("mneg_o", [128, G], F32)
    mf_o = sb("mf_o", [128, 2], F32)
    xdbl = sb("xdbl", [96, TQ], F32)

    def _(s):
        s.dma_start(out=ident[:, :], in_=io["ident"]).then_inc(*_dma(ctx, "qS"))
        for k in consts:
            s.dma_start(out=consts[k][:, :], in_=io[k]).then_inc(*_dma(ctx, "qS"))
    emit(nc, "sync", _)

    def _(g):
        g.dma_start(out=identb[:, :], in_=io["identb"]).then_inc(*_dma(ctx, "qG"))
    emit(nc, "gpsimd", _)

    def _(v):
        v.memset(cb[:, 0:1], EPS_RMS)
        v.memset(cb[:, 1:2], EPS_LN)
        v.memset(cb[:, 2:3], 1.0).then_inc(*_one(ctx, "eV"))
    emit(nc, "vector", _)
    cb_ready = ctx.v("eV")
    consts_loaded_s = ctx.v("qS")
    consts_loaded_g = ctx.v("qG")

    def _(sc):
        sc.wait_ge(ctx.s("qS"), consts_loaded_s)
        sc.activation(Amat[:, :], consts["A_log"][:, :], AF.Exp).then_inc(*_one(ctx, "eS"))
    emit(nc, "scalar", _)
    a_exp_done = ctx.v("eS")

    def _(v):
        v.wait_ge(ctx.s("eS"), a_exp_done)
        v.tensor_scalar_mul(Amat[:, :], Amat[:, :], -1.0).then_inc(*_one(ctx, "eV"))
    emit(nc, "vector", _)
    a_done = ctx.v("eV")

    with ExitStack() as ph:
        psb = lambda name, shape, dt: ph.enter_context(nc.sbuf_tensor(name, shape, dt))
        psp = lambda name, shape, dt: ph.enter_context(nc.psum_tensor(name, shape, dt))
        w_hi = psb("w_hi", [128, G, 2 * CH], BF16)
        w_lo = psb("w_lo", [128, G, 2 * CH], BF16)
        wxp_hi = psb("wxp_hi", [128, G, 96], BF16)
        wxp_lo = psb("wxp_lo", [128, G, 96], BF16)
        stage = psb("stage", [128, G, 512], F32)
        xt1 = psb("xt1", [128, 2, 1024], F32)
        ssq = psb("ssq", [128, 4], F32)
        rs = psb("rs", [128, 4], F32)
        nr1 = psb("nr1", [128, 4], F32)
        nr2 = psb("nr2", [128, 4], F32)
        r1f = psb("r1f", [128, 1024], F32)
        r1h = psb("r1h", [128, 1024], BF16)
        r1l = psb("r1l", [128, 1024], BF16)
        r1T_hi = psb("r1T_hi", [128, G, 512], BF16)
        r1T_lo = psb("r1T_lo", [128, G, 512], BF16)
        xpT = psb("xpT", [128, G, 516], F32)
        xc32 = psb("xc32", [128, G, 512], F32)
        xch = psb("xch", [128, G, 512], BF16)
        xcl = psb("xcl", [128, G, 512], BF16)
        xdbl_part = psb("xdbl_part", [96, TQ], F32)
        trash = r1f
        ps_trH = psp("ps_trH", [128, 1024], BF16)
        ps_trL = psp("ps_trL", [128, 1024], BF16)
        ps_m = [psp("ps_m0", [128, 512], F32), psp("ps_m1", [128, 512], F32)]
        ps_x = psp("ps_x", [96, 512], F32)

        wq_stage_done = []
        for q in range(4):
            def _(s, q=q):
                if q >= 1:
                    s.wait_ge(ctx.s("eV"), wq_stage_done[q - 1])
                src = bass.AP(io["w_in_t"].tensor, q * 512,
                              [[2 * CH, 128], [128 * 2 * CH, G], [1, 512]])
                s.dma_start(out=stage[:, :, :], in_=src).then_inc(*_dma(ctx, "qS"))
            emit(nc, "sync", _)
            sv = ctx.v("qS")

            def _(g, q=q):
                src = bass.AP(io["w_in_t"].tensor, q * 512,
                              [[2 * CH, 128], [128 * 2 * CH, G], [1, 512]])
                g.dma_start(out=w_hi[:, :, q * 512:(q + 1) * 512], in_=src).then_inc(*_dma(ctx, "qG"))
            emit(nc, "gpsimd", _)
            gv = ctx.v("qG")

            def _(v, q=q, sv=sv, gv=gv):
                v.wait_ge(ctx.s("qS"), sv)
                v.wait_ge(ctx.s("qG"), gv)
                v.tensor_tensor(w_lo[:, :, q * 512:(q + 1) * 512], stage[:, :, :],
                                w_hi[:, :, q * 512:(q + 1) * 512], op=OP.subtract).then_inc(*_one(ctx, "eV"))
            emit(nc, "vector", _)
            wq_stage_done.append(ctx.v("eV"))
        def _(s):
            s.wait_ge(ctx.s("eV"), wq_stage_done[3])
            src = bass.AP(io["w_xp_t"].tensor, 0, [[96, 128], [128 * 96, G], [1, 96]])
            s.dma_start(out=stage[:, :, 0:96], in_=src).then_inc(*_dma(ctx, "qS"))
        emit(nc, "sync", _)
        sv = ctx.v("qS")

        def _(g):
            src = bass.AP(io["w_xp_t"].tensor, 0, [[96, 128], [128 * 96, G], [1, 96]])
            g.dma_start(out=wxp_hi[:, :, :], in_=src).then_inc(*_dma(ctx, "qG"))
        emit(nc, "gpsimd", _)
        gv = ctx.v("qG")

        def _(v, sv=sv, gv=gv):
            v.wait_ge(ctx.s("qS"), sv)
            v.wait_ge(ctx.s("qG"), gv)
            v.tensor_tensor(wxp_lo[:, :, :], stage[:, :, 0:96], wxp_hi[:, :, :],
                            op=OP.subtract).then_inc(*_one(ctx, "eV"))
        emit(nc, "vector", _)
        w_ready = ctx.v("eV")
        ident_ready_g = consts_loaded_g

        def _(v):
            v.memset(xpT[:, :, 0:4], 0.0).then_inc(*_one(ctx, "eV"))
        emit(nc, "vector", _)
        pads_zeroed = ctx.v("eV")

        x_free_at = [None, None]
        tr_evac = {}
        spill_done = []
        sz_spill_hist = []
        prev_tau_evacs = None
        for tau in range(8):
            r1T_cols_ready = []
            for tt in range(4):
                par = tt % 2
                def _(s, tau=tau, tt=tt, par=par):
                    if x_free_at[par] is not None:
                        s.wait_ge(ctx.s("eV"), x_free_at[par])
                    s.dma_start(out=xt1[:, par, :],
                                in_=io["x_full"][tau * 512 + tt * 128:tau * 512 + tt * 128 + 128, :]
                                ).then_inc(*_dma(ctx, "qS"))
                emit(nc, "sync", _)
                xv = ctx.v("qS")

                def _(sc, tt=tt, par=par, xv=xv, tau=tau):
                    sc.wait_ge(ctx.s("qS"), xv)
                    if tau == 0 and tt == 0:
                        sc.wait_ge(ctx.s("eV"), cb_ready)
                    sc.activation(trash[:, :], xt1[:, par, :], AF.Square,
                                  accum_out=ssq[:, tt:tt + 1]).then_inc(*_one(ctx, "eS"))
                    sc.activation(nr1[:, tt:tt + 1], ssq[:, tt:tt + 1], AF.Ln,
                                  bias=cb[:, 0:1], scale=1.0 / 1024.0)
                    sc.activation(rs[:, tt:tt + 1], nr1[:, tt:tt + 1], AF.Exp,
                                  scale=-0.5).then_inc(*_one(ctx, "eS"))
                emit(nc, "scalar", _)
                sqv = ctx.v("eS")

                def _(v, tt=tt, par=par, sqv=sqv):
                    v.wait_ge(ctx.s("eS"), sqv)
                    c = slice(tt, tt + 1)
                    v.tensor_scalar(nr1[:, c], ssq[:, c], 1.0 / 1024.0, cb[:, 0:1], op0=OP.mult, op1=OP.add)
                    v.tensor_tensor(nr2[:, c], rs[:, c], rs[:, c], op=OP.mult)
                    v.tensor_tensor(nr2[:, c], nr2[:, c], nr1[:, c], op=OP.mult)
                    v.tensor_scalar(nr2[:, c], nr2[:, c], -0.5, 1.5, op0=OP.mult, op1=OP.add)
                    v.tensor_tensor(rs[:, c], rs[:, c], nr2[:, c], op=OP.mult)
                    v.tensor_scalar_mul(r1f[:, :], xt1[:, par, :], rs[:, c])
                    v.tensor_copy(r1h[:, :], r1f[:, :])
                    v.tensor_tensor(r1l[:, :], r1f[:, :], r1h[:, :], op=OP.subtract).then_inc(*_one(ctx, "eV"))
                emit(nc, "vector", _)
                x_free_at[par] = ctx.v("eV")
                splitv = ctx.v("eV")

                def _(t, tt=tt, splitv=splitv, tau=tau):
                    t.wait_ge(ctx.s("eV"), splitv)
                    t.wait_ge(ctx.s("qG"), ident_ready_g)
                    if (tau, tt) != (0, 0) and tr_evac.get("prev") is not None:
                        t.wait_ge(ctx.s("eV"), tr_evac["prev"])
                    last = None
                    for f in range(G):
                        last = t.transpose(ps_trH[:, f * 128:(f + 1) * 128], r1h[:, f * 128:(f + 1) * 128], identb[:, :])
                    for f in range(G):
                        last = t.transpose(ps_trL[:, f * 128:(f + 1) * 128], r1l[:, f * 128:(f + 1) * 128], identb[:, :])
                    last.then_inc(*_one(ctx, "eT"))
                emit(nc, "tensor", _)
                trv = ctx.v("eT")

                def _(v, tt=tt, trv=trv):
                    v.wait_ge(ctx.s("eT"), trv)
                    dsth = bass.AP(r1T_hi, tt * 128, [[G * 512, 128], [512, G], [1, 128]])
                    dstl = bass.AP(r1T_lo, tt * 128, [[G * 512, 128], [512, G], [1, 128]])
                    srch = bass.AP(ps_trH, 0, [[1024, 128], [128, G], [1, 128]])
                    srcl = bass.AP(ps_trL, 0, [[1024, 128], [128, G], [1, 128]])
                    v.tensor_copy(dsth, srch)
                    v.tensor_copy(dstl, srcl).then_inc(*_one(ctx, "eV"))
                emit(nc, "vector", _)
                tr_evac["prev"] = ctx.v("eV")
                r1T_cols_ready.append(ctx.v("eV"))

            evac_done = {}
            for of in range(16):
                bank = of % 2
                def _(t, of=of, bank=bank, rdy=r1T_cols_ready[-1], tau=tau):
                    if of < 2 and tau == 0:
                        t.wait_ge(ctx.s("eV"), max(rdy, w_ready))
                    elif of < 2:
                        t.wait_ge(ctx.s("eV"), rdy)
                        t.wait_ge(ctx.s("eV"), prev_tau_evacs[of][0])
                        t.wait_ge(ctx.s("eS"), prev_tau_evacs[of][1])
                    if evac_done.get(of - 2) is not None:
                        t.wait_ge(ctx.s("eV"), evac_done[of - 2][0])
                        t.wait_ge(ctx.s("eS"), evac_done[of - 2][1])
                    last = None
                    for k in range(G):
                        last = mm3(t, ps_m[bank][:, :],
                                   w_hi[:, k, of * 128:(of + 1) * 128], w_lo[:, k, of * 128:(of + 1) * 128],
                                   r1T_hi[:, k, :], r1T_lo[:, k, :],
                                   start=(k == 0), stop=(k == G - 1))
                    last.then_inc(*_one(ctx, "eT"))
                emit(nc, "tensor", _)
                mmv = ctx.v("eT")

                if of < 8:
                    def _(v, of=of, bank=bank, mmv=mmv):
                        v.wait_ge(ctx.s("eT"), mmv)
                        v.tensor_copy(xpT[:, of, 4:516], ps_m[bank][:, :]).then_inc(*_one(ctx, "eV"))
                    emit(nc, "vector", _)
                    evac_done[of] = (ctx.v("eV"), ctx.v("eS"))
                else:
                    def _(sc, of=of, bank=bank, mmv=mmv, tau=tau):
                        sc.wait_ge(ctx.s("eT"), mmv)
                        if of == 8:
                            if tau == 0:
                                sc.wait_ge(ctx.s("eV"), w_ready)
                            else:
                                sc.wait_ge(ctx.s("qS"), sz_spill_hist[-1])
                        sc.activation(stage[:, of - 8, :], ps_m[bank][:, :], AF.Silu).then_inc(*_one(ctx, "eS"))
                    emit(nc, "scalar", _)
                    evac_done[of] = (ctx.v("eV"), ctx.v("eS"))
            z_done = ctx.v("eS")
            prev_tau_evacs = (evac_done[14], evac_done[15])

            def _(s, tau=tau, z_done=z_done):
                s.wait_ge(ctx.s("eS"), z_done)
                dst = bass.AP(io["sp_sz"].tensor, tau * 512, [[TQ, 128], [128 * TQ, G], [1, 512]])
                s.dma_start(out=dst, in_=stage[:, :, :]).then_inc(*_dma(ctx, "qS"))
            emit(nc, "sync", _)
            sz_spill_hist.append(ctx.v("qS"))

            xp_done = evac_done[6][0]
            xp_done = max(evac_done[i][0] for i in range(8))
            def _(v, tau=tau, xp_done=xp_done):
                v.wait_ge(ctx.s("eV"), xp_done)
                if tau == 0:
                    v.wait_ge(ctx.s("eV"), pads_zeroed)
                v.wait_ge(ctx.s("qS"), consts_loaded_s)
                last = None
                for g_ in range(G):
                    v.tensor_scalar_mul(xc32[:, g_, :], xpT[:, g_, 1:513], consts["conv_w"][:, g_ * 4 + 0:g_ * 4 + 1])
                    for k in range(1, 4):
                        v.scalar_tensor_tensor(xc32[:, g_, :], xpT[:, g_, 1 + k:513 + k],
                                               consts["conv_w"][:, g_ * 4 + k:g_ * 4 + k + 1],
                                               xc32[:, g_, :], op0=OP.mult, op1=OP.add)
                    last = v.tensor_copy(xpT[:, g_, 1:4], xpT[:, g_, 513:516])
                last.then_inc(*_one(ctx, "eV"))
            emit(nc, "vector", _)
            conv_done = ctx.v("eV")

            def _(sc, conv_done=conv_done):
                sc.wait_ge(ctx.s("eV"), conv_done)
                last = None
                for g_ in range(G):
                    last = sc.activation(xc32[:, g_, :], xc32[:, g_, :], AF.Silu,
                                         bias=consts["conv_b"][:, g_:g_ + 1])
                last.then_inc(*_one(ctx, "eS"))
            emit(nc, "scalar", _)
            silu_done = ctx.v("eS")

            def _(v, silu_done=silu_done, tau=tau):
                v.wait_ge(ctx.s("eS"), silu_done)
                if tau >= 1:
                    v.wait_ge(ctx.s("qS"), spill_done[-1])
                v.tensor_copy(xch[:, :, :], xc32[:, :, :])
                v.tensor_tensor(xcl[:, :, :], xc32[:, :, :], xch[:, :, :], op=OP.subtract).then_inc(*_one(ctx, "eV"))
            emit(nc, "vector", _)
            xc_split_done = ctx.v("eV")

            def _(s, tau=tau, xc_split_done=xc_split_done):
                s.wait_ge(ctx.s("eV"), xc_split_done)
                dsth = bass.AP(io["sp_xc_hi"].tensor, tau * 512, [[TQ, 128], [128 * TQ, G], [1, 512]])
                dstl = bass.AP(io["sp_xc_lo"].tensor, tau * 512, [[TQ, 128], [128 * TQ, G], [1, 512]])
                s.dma_start(out=dsth, in_=xch[:, :, :]).then_inc(*_dma(ctx, "qS"))
                s.dma_start(out=dstl, in_=xcl[:, :, :]).then_inc(*_dma(ctx, "qS"))
            emit(nc, "sync", _)
            spill_done.append(ctx.v("qS"))

            def _(t, tau=tau, xc_split_done=xc_split_done):
                t.wait_ge(ctx.s("eV"), xc_split_done)
                if tr_evac.get("psx_free") is not None:
                    t.wait_ge(ctx.s("eV"), tr_evac["psx_free"])
                last = None
                for k in range(G):
                    last = mm3(t, ps_x[:, :], wxp_hi[:, k, :], wxp_lo[:, k, :],
                               xch[:, k, :], xcl[:, k, :], start=(k == 0), stop=(k == G - 1))
                last.then_inc(*_one(ctx, "eT"))
            emit(nc, "tensor", _)
            xpv = ctx.v("eT")

            def _(v, tau=tau, xpv=xpv):
                v.wait_ge(ctx.s("eT"), xpv)
                v.tensor_copy(xdbl_part[:, tau * 512:(tau + 1) * 512], ps_x[:, :]).then_inc(*_one(ctx, "eV"))
            emit(nc, "vector", _)
            tr_evac["psx_free"] = ctx.v("eV")
        xdbl_ready = ctx.v("eV")

        def _(s, xdbl_ready=xdbl_ready):
            s.wait_ge(ctx.s("eV"), xdbl_ready)
            s.dma_start(out=io["ar_in"], in_=xdbl_part[:, :]).then_inc(*_dma(ctx, "qS"))
        emit(nc, "sync", _)
        arin_v = ctx.v("qS")

        def _(g, arin_v=arin_v):
            g.wait_ge(ctx.s("qS"), arin_v)
            g.collective_compute(
                "AllReduce", OP.add, replica_groups=[[0, 1], [2, 3], [4, 5], [6, 7]],
                ins=[io["ar_in"].opt()], outs=[io["ar_out"].opt()],
            ).then_inc(*_one(ctx, "CC"))
        emit(nc, "gpsimd", _)
        cc1 = ctx.v("CC")

        def _(s, cc1=cc1):
            s.wait_ge(ctx.s("CC"), cc1)
            s.dma_start(out=xdbl[:, :], in_=io["ar_out"]).then_inc(*_dma(ctx, "qS"))
        emit(nc, "sync", _)
        xdbl_in_v = ctx.v("qS")

        def _(s, xdbl_in_v=xdbl_in_v):
            s.wait_ge(ctx.s("qS"), xdbl_in_v)
            s.dma_start(out=io["sp_bc"], in_=xdbl[64:96, :]).then_inc(*_dma(ctx, "qS"))
        emit(nc, "sync", _)
        bc_spill_v = ctx.v("qS")

        dtr_ready = None

        if "mamba1" in debug:
            def _(s):
                s.wait_ge(ctx.s("qS"), max(max(spill_done), bc_spill_v))
                s.dma_start(out=dbg["xdbl"], in_=xdbl[:, :]).then_inc(*_dma(ctx, "qS"))
                s.dma_start(out=dbg["sz"], in_=io["sp_sz"]).then_inc(*_dma(ctx, "qS"))
                s.dma_start(out=dbg["xc_hi"], in_=io["sp_xc_hi"]).then_inc(*_dma(ctx, "qS"))
                s.dma_start(out=dbg["xc_lo"], in_=io["sp_xc_lo"]).then_inc(*_dma(ctx, "qS"))
                s.wait_ge(ctx.s("qS"), ctx.v("qS"))
            emit(nc, "sync", _)
    state = dict(ctx=ctx, io=io, nc=nc, consts=consts, Amat=Amat, xdbl=xdbl, cb=cb,
                 mneg=mneg_o, mf=mf_o,
                 spill_done=spill_done, bc_spill_v=bc_spill_v,
                 xdbl_in_v=xdbl_in_v, consts_loaded_s=consts_loaded_s,
                 ident=ident, identb=identb, outer=outer)
    if "stop1" in debug:
        finish_debug(state)
        return nc, state
    build_ph4(ctx, state, debug)
    if "stop2" in debug:
        finish_debug(state)
        return nc, state
    build_ph5(ctx, state, debug)
    if "stop3" in debug:
        finish_debug(state)
        return nc, state
    build_ph67(ctx, state, debug)
    return nc, state


def finish_debug(state):
    ctx, nc, io = state["ctx"], state["nc"], state["io"]
    z = nc.alloc_sbuf_tensor("zero_fill", [128, DM], F32)

    def _(v):
        v.memset(z[:, :], 0.0).then_inc(*_one(ctx, "eV"))
    emit(nc, "vector", _)
    zv = ctx.v("eV")

    def _(s, zv=zv):
        s.wait_ge(ctx.s("eV"), zv)
        for r in range(TO // 128):
            s.dma_start(out=io["y_out"][r * 128:(r + 1) * 128, :], in_=z[:, :]).then_inc(*_dma(ctx, "qO"))
        s.wait_ge(ctx.s("qO"), ctx.v("qO"))
    emit(nc, "sync", _)


def build_ph4(ctx, state, debug):
    nc, io, consts = state["nc"], state["io"], state["consts"]
    Amat, xdbl, cb = state["Amat"], state["xdbl"], state["cb"]
    from contextlib import ExitStack
    dbg = {}
    if "mamba2" in debug:
        dbg["dt"] = nc.dram_tensor("dbg_dt", [CH, TQ], F32, kind="ExternalOutput").ap()
        dbg["ypz"] = nc.dram_tensor("dbg_ypz", [CH, TQ], F32, kind="ExternalOutput").ap()
        dbg["yf_hi"] = nc.dram_tensor("dbg_yf_hi", [CH, TQ], BF16, kind="ExternalOutput").ap()
        dbg["yf_lo"] = nc.dram_tensor("dbg_yf_lo", [CH, TQ], BF16, kind="ExternalOutput").ap()
        dbg["mmp"] = nc.dram_tensor("dbg_mmp", [DM, TQ], F32, kind="ExternalOutput").ap()
        io["sp_yf_hi"] = nc.dram_tensor("sp_yf_hi", [CH, TQ], BF16).ap()
        io["sp_yf_lo"] = nc.dram_tensor("sp_yf_lo", [CH, TQ], BF16).ap()

    with ExitStack() as ph:
        psb = lambda name, shape, dt: ph.enter_context(nc.sbuf_tensor(name, shape, dt))
        psp = lambda name, shape, dt: ph.enter_context(nc.psum_tensor(name, shape, dt))
        dtr_hi = psb("dtr_hi", [64, TQ], BF16)
        dtr_lo = psb("dtr_lo", [64, TQ], BF16)
        wdt_hi = psb("wdt_hi", [64, CH], BF16)
        wdt_lo = psb("wdt_lo", [64, CH], BF16)
        wout_hi = psb("wout_hi", [128, G, DM], BF16)
        wout_lo = psb("wout_lo", [128, G, DM], BF16)
        wstage = psb("wstage", [128, G, 512], F32)
        dt_b_ = psb("dt_b_", [128, G, 512], F32)
        du_b = psb("du_b", [128, G, 512], F32)
        y_b = psb("y_b", [128, G, 512], F32)
        yf_hi = psb("yf_hi", [128, G, 512], BF16)
        yf_lo = psb("yf_lo", [128, G, 512], BF16)
        brep = psb("brep", [128, 2, 512], F32)
        crep = psb("crep", [128, 2, 512], F32)
        dA = psb("dA_", [128, 2, 512], F32)
        wB = psb("wB_", [128, 2, 512], F32)
        carry = psb("carry", [128, 128], F32)
        ug = psb("ug", [128, 2, 512], F32)
        uh = psb("uh", [128, 2, 512], BF16)
        ul = psb("ul", [128, 2, 512], BF16)
        szg = psb("szg", [128, 2, 512], F32)
        tmp1 = psb("tmp1", [128, 512], F32)
        tmp2 = psb("tmp2", [128, 512], F32)
        mmp = psb("mmp", [128, G, 512], F32)
        ps_m = [psp("p4_m0", [128, 512], F32), psp("p4_m1", [128, 512], F32)]

        def _(v):
            v.wait_ge(ctx.s("qS"), state["xdbl_in_v"])
            v.tensor_copy(dtr_hi[:, :], xdbl[0:64, :])
            v.tensor_tensor(dtr_lo[:, :], xdbl[0:64, :], dtr_hi[:, :], op=OP.subtract)
            v.memset(carry[:, :], 0.0).then_inc(*_one(ctx, "eV"))
        emit(nc, "vector", _)
        dtr_ready = ctx.v("eV")

        def _(s):
            s.dma_start(out=wstage[0:64, 0:2, :], in_=io["w_dt_t"]).then_inc(*_dma(ctx, "qS"))
        emit(nc, "sync", _)
        sv = ctx.v("qS")

        def _(g):
            g.dma_start(out=wdt_hi[:, :], in_=io["w_dt_t"]).then_inc(*_dma(ctx, "qG"))
        emit(nc, "gpsimd", _)
        gv = ctx.v("qG")

        def _(v, sv=sv, gv=gv):
            v.wait_ge(ctx.s("qS"), sv)
            v.wait_ge(ctx.s("qG"), gv)
            wsv = bass.AP(wstage, 0, [[G * 512, 64], [1, 1024]])
            v.tensor_tensor(wdt_lo[:, :], wsv, wdt_hi[:, :], op=OP.subtract).then_inc(*_one(ctx, "eV"))
        emit(nc, "vector", _)
        wdt_ready = ctx.v("eV")

        wout_ready = None
        for q in range(2):
            def _(s, q=q):
                if q == 1:
                    s.wait_ge(ctx.s("eV"), wout_ready)
                src = bass.AP(io["w_out_t"].tensor, q * 512, [[DM, 128], [128 * DM, G], [1, 512]])
                s.dma_start(out=wstage[:, :, :], in_=src).then_inc(*_dma(ctx, "qS"))
            emit(nc, "sync", _)
            sv = ctx.v("qS")

            def _(g, q=q):
                src = bass.AP(io["w_out_t"].tensor, q * 512, [[DM, 128], [128 * DM, G], [1, 512]])
                dst = bass.AP(wout_hi, q * 512, [[G * DM, 128], [DM, G], [1, 512]])
                g.dma_start(out=dst, in_=src).then_inc(*_dma(ctx, "qG"))
            emit(nc, "gpsimd", _)
            gv = ctx.v("qG")

            def _(v, q=q, sv=sv, gv=gv):
                v.wait_ge(ctx.s("qS"), sv)
                v.wait_ge(ctx.s("qG"), gv)
                dsth = bass.AP(wout_hi, q * 512, [[G * DM, 128], [DM, G], [1, 512]])
                dstl = bass.AP(wout_lo, q * 512, [[G * DM, 128], [DM, G], [1, 512]])
                v.tensor_tensor(dstl, wstage[:, :, :], dsth, op=OP.subtract).then_inc(*_one(ctx, "eV"))
            emit(nc, "vector", _)
            wout_ready = ctx.v("eV")

        meta_sb = consts["meta"]
        jreg_box = {}

        def _(g):
            g.wait_ge(ctx.s("qS"), state["consts_loaded_s"])
            jreg_box["j"] = g.value_load(meta_sb[0:1, 0:1])
        emit(nc, "gpsimd", _)
        state["jreg"] = jreg_box["j"]
        exp_done = {}
        hC_done = {}
        unit = 0
        brep_free = {}
        ug_free = {}
        sz_free = {}
        dt_evac_hist = {}
        mm_spill = []
        mm_done_prev = None
        for hq in range(8):
            col = slice(hq * 512, (hq + 1) * 512)
            dt_ready = {}
            for g_ in range(G):
                bank = g_ % 2
                def _(t, g_=g_, bank=bank, hq=hq):
                    if hq == 0 and g_ == 0:
                        t.wait_ge(ctx.s("eV"), max(dtr_ready, wdt_ready, wout_ready))
                    if g_ < 2 and hq >= 1:
                        t.wait_ge(ctx.s("eV"), mm_done_prev)
                    if g_ >= 2:
                        t.wait_ge(ctx.s("eS"), dt_evac_hist[(hq, g_ - 2)])
                    last = mm3(t, ps_m[bank][:, :], wdt_hi[:, g_ * 128:(g_ + 1) * 128],
                               wdt_lo[:, g_ * 128:(g_ + 1) * 128],
                               dtr_hi[:, col], dtr_lo[:, col], start=True, stop=True)
                    last.then_inc(*_one(ctx, "eT"))
                emit(nc, "tensor", _)
                mmv = ctx.v("eT")

                def _(sc, g_=g_, bank=bank, mmv=mmv):
                    sc.wait_ge(ctx.s("eT"), mmv)
                    sc.activation(tmp1[:, :], ps_m[bank][:, :], AF.Abs, bias=consts["dt_b"][:, g_:g_ + 1])
                    sc.activation(tmp1[:, :], tmp1[:, :], AF.Exp, scale=-1.0)
                    sc.activation(tmp1[:, :], tmp1[:, :], AF.Ln, bias=cb[:, 2:3])
                    sc.activation(tmp2[:, :], ps_m[bank][:, :], AF.Relu,
                                  bias=consts["dt_b"][:, g_:g_ + 1]).then_inc(*_one(ctx, "eS"))
                emit(nc, "scalar", _)
                spv = ctx.v("eS")
                dt_evac_hist[(hq, g_)] = spv

                def _(v, g_=g_, spv=spv):
                    v.wait_ge(ctx.s("eS"), spv)
                    v.tensor_tensor(dt_b_[:, g_, :], tmp1[:, :], tmp2[:, :], op=OP.add).then_inc(*_one(ctx, "eV"))
                emit(nc, "vector", _)
                dt_ready[g_] = ctx.v("eV")

                par = g_ % 2
                def _(s, g_=g_, hq=hq, par=par):
                    if ug_free.get(par) is not None:
                        s.wait_ge(ctx.s("eV"), ug_free[par])
                    srch = bass.AP(io["sp_xc_hi"].tensor, g_ * 128 * TQ + hq * 512, [[TQ, 128], [1, 512]])
                    srcl = bass.AP(io["sp_xc_lo"].tensor, g_ * 128 * TQ + hq * 512, [[TQ, 128], [1, 512]])
                    s.dma_start(out=uh[:, par, :], in_=srch).then_inc(*_dma(ctx, "qS"))
                    s.dma_start(out=ul[:, par, :], in_=srcl).then_inc(*_dma(ctx, "qS"))
                emit(nc, "sync", _)
                uv = ctx.v("qS")

                def _(v, g_=g_, par=par, uv=uv, dtv=dt_ready[g_]):
                    v.wait_ge(ctx.s("qS"), uv)
                    v.tensor_tensor(ug[:, par, :], uh[:, par, :], ul[:, par, :], op=OP.add)
                    v.tensor_tensor(du_b[:, g_, :], dt_b_[:, g_, :], ug[:, par, :], op=OP.mult)
                    v.tensor_scalar_mul(y_b[:, g_, :], ug[:, par, :], consts["Dvec"][:, g_:g_ + 1]).then_inc(*_one(ctx, "eV"))
                emit(nc, "vector", _)
                ug_free[par] = ctx.v("eV")
            du_ready = ctx.v("eV")

            for s_ in range(S):
                spar = s_ % 2
                def _(s, s_=s_, spar=spar, hq=hq):
                    if brep_free.get(spar) is not None:
                        s.wait_ge(ctx.s("eV"), brep_free[spar])
                    srcb = bass.AP(io["sp_bc"].tensor, s_ * TQ + hq * 512, [[0, 128], [1, 512]])
                    srcc = bass.AP(io["sp_bc"].tensor, (16 + s_) * TQ + hq * 512, [[0, 128], [1, 512]])
                    s.dma_start(out=brep[:, spar, :], in_=srcb).then_inc(*_dma(ctx, "qS"))
                    s.dma_start(out=crep[:, spar, :], in_=srcc).then_inc(*_dma(ctx, "qS"))
                emit(nc, "sync", _)
                bv = ctx.v("qS")

                for g_ in range(G):
                    p = unit % 2
                    def _(sc, g_=g_, s_=s_, p=p, hq=hq):
                        if hC_done.get(unit - 2) is not None:
                            sc.wait_ge(ctx.s("eV"), hC_done[unit - 2])
                        sc.wait_ge(ctx.s("eV"), du_ready)
                        sc.activation(dA[:, p, :], dt_b_[:, g_, :], AF.Exp,
                                      scale=Amat[:, g_ * S + s_:g_ * S + s_ + 1]).then_inc(*_one(ctx, "eS"))
                    emit(nc, "scalar", _)
                    exp_done[unit] = ctx.v("eS")

                    def _(v, g_=g_, s_=s_, p=p, spar=spar, bv=bv, ev=exp_done[unit], unit=unit):
                        v.wait_ge(ctx.s("qS"), bv)
                        v.tensor_tensor(wB[:, p, :], du_b[:, g_, :], brep[:, spar, :], op=OP.mult)
                        v.wait_ge(ctx.s("eS"), ev)
                        ci = g_ * S + s_
                        v.tensor_tensor_scan(dA[:, p, :], dA[:, p, :], wB[:, p, :],
                                             carry[:, ci:ci + 1], op0=OP.mult, op1=OP.add).then_inc(*_one(ctx, "eV"))
                        v.tensor_tensor(wB[:, p, :], dA[:, p, :], crep[:, spar, :], op=OP.mult)
                        v.tensor_tensor(y_b[:, g_, :], y_b[:, g_, :], wB[:, p, :], op=OP.add).then_inc(*_one(ctx, "eV"))
                    emit(nc, "vector", _)
                    scan_done_v = ctx.v("eV") - 1
                    hC_done[unit] = ctx.v("eV")

                    def _(sc, g_=g_, s_=s_, p=p, sdv=scan_done_v):
                        sc.wait_ge(ctx.s("eV"), sdv)
                        ci = g_ * S + s_
                        sc.copy(carry[:, ci:ci + 1], dA[:, p, 511:512]).then_inc(*_one(ctx, "eS"))
                    emit(nc, "scalar", _)
                    if g_ == G - 1:
                        brep_free[spar] = ctx.v("eV")
                    unit += 1
            yacc_done = ctx.v("eV")

            if "mamba2" in debug:
                def _(s, hq=hq, yacc_done=yacc_done):
                    s.wait_ge(ctx.s("eV"), yacc_done)
                    dstd = bass.AP(dbg["dt"].tensor, hq * 512, [[TQ, 128], [128 * TQ, G], [1, 512]])
                    dsty = bass.AP(dbg["ypz"].tensor, hq * 512, [[TQ, 128], [128 * TQ, G], [1, 512]])
                    s.dma_start(out=dstd, in_=dt_b_[:, :, :]).then_inc(*_dma(ctx, "qS"))
                    s.dma_start(out=dsty, in_=y_b[:, :, :]).then_inc(*_dma(ctx, "qS"))
                emit(nc, "sync", _)
                dbg_y_v = ctx.v("qS")
            else:
                dbg_y_v = None

            for g_ in range(G):
                par = g_ % 2
                def _(s, g_=g_, hq=hq, par=par):
                    if sz_free.get(par) is not None:
                        s.wait_ge(ctx.s("eV"), sz_free[par])
                    src = bass.AP(io["sp_sz"].tensor, g_ * 128 * TQ + hq * 512, [[TQ, 128], [1, 512]])
                    s.dma_start(out=szg[:, par, :], in_=src).then_inc(*_dma(ctx, "qS"))
                emit(nc, "sync", _)
                sv2 = ctx.v("qS")

                def _(v, g_=g_, par=par, sv2=sv2, yd=yacc_done, dbg_y_v=dbg_y_v):
                    v.wait_ge(ctx.s("qS"), sv2)
                    if dbg_y_v is not None and g_ == 0:
                        v.wait_ge(ctx.s("qS"), dbg_y_v)
                    v.tensor_tensor(y_b[:, g_, :], y_b[:, g_, :], szg[:, par, :], op=OP.mult)
                    v.tensor_copy(yf_hi[:, g_, :], y_b[:, g_, :])
                    v.tensor_tensor(yf_lo[:, g_, :], y_b[:, g_, :], yf_hi[:, g_, :], op=OP.subtract).then_inc(*_one(ctx, "eV"))
                emit(nc, "vector", _)
                sz_free[par] = ctx.v("eV")
            yf_ready = ctx.v("eV")

            if "mamba2" in debug:
                def _(s, hq=hq, yf_ready=yf_ready):
                    s.wait_ge(ctx.s("eV"), yf_ready)
                    dsth = bass.AP(io["sp_yf_hi"].tensor, hq * 512, [[TQ, 128], [128 * TQ, G], [1, 512]])
                    dstl = bass.AP(io["sp_yf_lo"].tensor, hq * 512, [[TQ, 128], [128 * TQ, G], [1, 512]])
                    s.dma_start(out=dsth, in_=yf_hi[:, :, :]).then_inc(*_dma(ctx, "qS"))
                    s.dma_start(out=dstl, in_=yf_lo[:, :, :]).then_inc(*_dma(ctx, "qS"))
                emit(nc, "sync", _)

            op_evac = {}
            for f in range(G):
                bank = f % 2
                def _(t, f=f, bank=bank, yf_ready=yf_ready, hq=hq):
                    if f < 2:
                        t.wait_ge(ctx.s("eV"), yf_ready)
                    if op_evac.get(f - 2) is not None:
                        t.wait_ge(ctx.s("eV"), op_evac[f - 2])
                    last = None
                    for k in range(G):
                        last = mm3(t, ps_m[bank][:, :], wout_hi[:, k, f * 128:(f + 1) * 128],
                                   wout_lo[:, k, f * 128:(f + 1) * 128],
                                   yf_hi[:, k, :], yf_lo[:, k, :], start=(k == 0), stop=(k == G - 1))
                    last.then_inc(*_one(ctx, "eT"))
                emit(nc, "tensor", _)
                mmv = ctx.v("eT")

                def _(v, f=f, bank=bank, mmv=mmv, hq=hq):
                    v.wait_ge(ctx.s("eT"), mmv)
                    if f == 0 and hq >= 1:
                        v.wait_ge(ctx.s(mm_spill[-1][0]), mm_spill[-1][1])
                    v.tensor_copy(mmp[:, f, :], ps_m[bank][:, :]).then_inc(*_one(ctx, "eV"))
                emit(nc, "vector", _)
                op_evac[f] = ctx.v("eV")
            mm_done = ctx.v("eV")
            mm_done_prev = mm_done

            def _(s, hq=hq, mm_done=mm_done):
                s.wait_ge(ctx.s("eV"), mm_done)
                dst = bass.AP(io["sp_mmp"].tensor, hq * DM * 512, [[512, 128], [128 * 512, G], [1, 512]])
                s.dma_start(out=dst, in_=mmp[:, :, :]).then_inc(*_dma(ctx, "qS"))
            emit(nc, "sync", _)
            mm_spill.append(("qS", ctx.v("qS")))
        def _(g):
            g.wait_ge(ctx.s("qS"), mm_spill[-1][1])
            for o in range(2):
                base = io["sh_mm"][o, state["jreg"], 0, :, :]
                dst = bass.AP(base.tensor, base.offset, [[DM * 512, 4], [1, DM * 512]])
                srcv = bass.AP(io["sp_mmp"].tensor, o * DM * 512, [[2 * DM * 512, 4], [1, DM * 512]])
                g.dma_start(out=dst, in_=srcv, bounds_check=None).then_inc(*_dma(ctx, "qG"))
        emit(nc, "gpsimd", _)
        mm_spill = [("qG", ctx.v("qG"))]
        state["mm_spill"] = mm_spill
        state["carry_final"] = None

        if "mamba2" in debug:
            def _(s):
                s.wait_ge(ctx.s("qS"), ctx.v("qS"))
                s.wait_ge(ctx.s("qG"), ctx.v("qG"))
                s.dma_start(out=dbg["yf_hi"], in_=io["sp_yf_hi"]).then_inc(*_dma(ctx, "qS"))
                s.dma_start(out=dbg["yf_lo"], in_=io["sp_yf_lo"]).then_inc(*_dma(ctx, "qS"))
                s.wait_ge(ctx.s("qS"), ctx.v("qS"))
            emit(nc, "sync", _)

            def _(s):
                s.wait_ge(ctx.s("qG"), mm_spill[0][1])
                for hq in range(8):
                    src_ = bass.AP(io["sp_mmp"].tensor, hq * DM * 512, [[512, 128], [128 * 512, G], [1, 512]])
                    dst_ = bass.AP(dbg["mmp"].tensor, hq * 512, [[TQ, 128], [128 * TQ, G], [1, 512]])
                    s.dma_start(out=mmp[:, :, :], in_=src_).then_inc(*_dma(ctx, "qS"))
                    s.wait_ge(ctx.s("qS"), ctx.v("qS"))
                    s.dma_start(out=dst_, in_=mmp[:, :, :]).then_inc(*_dma(ctx, "qS"))
                    s.wait_ge(ctx.s("qS"), ctx.v("qS"))
            emit(nc, "sync", _)
    state["ph4_dbg"] = dbg
    return dbg


def build_ph5(ctx, state, debug):
    nc, io, consts = state["nc"], state["io"], state["consts"]
    cb = state["cb"]
    identf = state["ident"]
    from contextlib import ExitStack
    dbg = {}
    if "ph5" in debug:
        dbg["x4"] = nc.dram_tensor("dbg_x4", [DM, TO], F32, kind="ExternalOutput").ap()
        dbg["kqh"] = nc.dram_tensor("dbg_kqh", [DM, TO], BF16, kind="ExternalOutput").ap()
        dbg["kql"] = nc.dram_tensor("dbg_kql", [DM, TO], BF16, kind="ExternalOutput").ap()
        dbg["vt"] = nc.dram_tensor("dbg_vt", [TO, DM], F32, kind="ExternalOutput").ap()
        dbg["mst"] = nc.dram_tensor("dbg_mst", [128, G], F32, kind="ExternalOutput").ap()
        dbg["stat"] = nc.dram_tensor("dbg_stat", [1, 64], F32, kind="ExternalOutput").ap()
        dbg["rowt"] = nc.dram_tensor("dbg_rowt", [1, 6, 512], F32, kind="ExternalOutput").ap()
    io["sp_h1h"] = nc.dram_tensor("sp_h1h", [4 * DM, TO], BF16).ap()
    io["sp_h1l"] = nc.dram_tensor("sp_h1l", [4 * DM, TO], BF16).ap()
    io["sp_w2h"] = nc.dram_tensor("sp_w2h", [4 * DM, DM], BF16).ap()
    io["sp_w2l"] = nc.dram_tensor("sp_w2l", [4 * DM, DM], BF16).ap()
    io["sp_x4"] = nc.dram_tensor("sp_x4", [DM, TO], F32).ap()
    io["sp_rep"] = nc.dram_tensor("sp_rep", [1, 2048], F32).ap()
    jreg = state["jreg"]
    SC = 1.0 / 32.0

    with ExitStack() as ph:
        psb = lambda name, shape, dt: ph.enter_context(nc.sbuf_tensor(name, shape, dt))
        psp = lambda name, shape, dt: ph.enter_context(nc.psum_tensor(name, shape, dt))
        bufA = psb("bufA", [128, G, 512], F32)
        bufB = psb("bufB", [128, G, 512], F32)
        bufC = psb("bufC", [128, G, 512], F32)
        r2h = psb("r2h", [128, G, 512], BF16)
        r2l = psb("r2l", [128, G, 512], BF16)
        xnh = psb("xnh", [128, G, 512], BF16)
        xnl = psb("xnl", [128, G, 512], BF16)
        wst = psb("wst", [128, G, 128], F32)
        wh = psb("wh5", [128, 2, G, 128], BF16)
        wl = psb("wl5", [128, 2, G, 128], BF16)
        h1h = psb("h1h", [128, 2, 512], BF16)
        h1l = psb("h1l", [128, 2, 512], BF16)
        t32a = psb("t32a", [128, 512], F32)
        t32b = psb("t32b", [128, 512], F32)
        rowt = psb("rowt", [1, 6, 512], F32)
        rep1 = psb("rep5a", [128, 512], F32)
        rep2 = psb("rep5b", [128, 512], F32)
        bv_rep = psb("bv_rep", [128, DM], F32)
        wvh = psb("wvh", [128, G, DM], BF16)
        wvl = psb("wvl", [128, G, DM], BF16)
        stat = psb("stat", [1, 64], F32)
        xtile = psb("xtile5", [128, 1024], F32)
        sqb = psb("sqb", [128, 2, 512], F32)
        kst_h = psb("kst_h", [128, G, 512], BF16)
        kst_l = psb("kst_l", [128, G, 512], BF16)
        mf = state["mf"]
        mneg = state["mneg"]
        ps5 = [psp(f"p5_{k}", [128, 512], F32) for k in range(8)]

        def _(s):
            s.dma_start(out=bv_rep[:, :],
                        in_=bass.AP(io["b_v_row"].tensor, 0, [[0, 128], [1, DM]])).then_inc(*_dma(ctx, "qS"))
            s.dma_start(out=mf[:, :], in_=io["meta_f"]).then_inc(*_dma(ctx, "qS"))
        emit(nc, "sync", _)
        bv_v = ctx.v("qS")

        def _(v, bv_v=bv_v):
            v.wait_ge(ctx.s("qS"), bv_v)
            v.memset(stat[:, :], 0.0).then_inc(*_one(ctx, "eV"))
        emit(nc, "vector", _)
        stat_0 = ctx.v("eV")

        w2_split_done = []
        for q in range(8):
            def _(s, q=q):
                if q >= 1:
                    s.wait_ge(ctx.s("eV"), w2_split_done[q - 1])
                src = bass.AP(io["w_m2_t"].tensor, q * 512 * DM,
                              [[DM, 128], [128 * DM, 4], [1, DM]])
                dst = bass.AP(bufA, 0, [[G * 512, 128], [DM, 4], [1, DM]])
                s.dma_start(out=dst, in_=src).then_inc(*_dma(ctx, "qS"))
            emit(nc, "sync", _)
            sv = ctx.v("qS")

            def _(g, q=q):
                src = bass.AP(io["w_m2_t"].tensor, q * 512 * DM,
                              [[DM, 128], [128 * DM, 4], [1, DM]])
                dst = bass.AP(r2h, 0, [[G * 512, 128], [DM, 4], [1, DM]])
                g.dma_start(out=dst, in_=src).then_inc(*_dma(ctx, "qG"))
            emit(nc, "gpsimd", _)
            gv = ctx.v("qG")

            def _(v, q=q, sv=sv, gv=gv):
                v.wait_ge(ctx.s("qS"), sv)
                v.wait_ge(ctx.s("qG"), gv)
                srch = bass.AP(r2h, 0, [[G * 512, 128], [DM, 4], [1, DM]])
                dstl = bass.AP(r2l, 0, [[G * 512, 128], [DM, 4], [1, DM]])
                stg = bass.AP(bufA, 0, [[G * 512, 128], [DM, 4], [1, DM]])
                v.tensor_tensor(dstl, stg, srch, op=OP.subtract).then_inc(*_one(ctx, "eV"))
            emit(nc, "vector", _)
            spl = ctx.v("eV")
            w2_split_done.append(spl)

            def _(s, q=q, spl=spl):
                s.wait_ge(ctx.s("eV"), spl)
                srch = bass.AP(r2h, 0, [[G * 512, 128], [DM, 4], [1, DM]])
                srcl = bass.AP(r2l, 0, [[G * 512, 128], [DM, 4], [1, DM]])
                dsth = bass.AP(io["sp_w2h"].tensor, q * 512 * DM, [[DM, 128], [128 * DM, 4], [1, DM]])
                dstl = bass.AP(io["sp_w2l"].tensor, q * 512 * DM, [[DM, 128], [128 * DM, 4], [1, DM]])
                s.dma_start(out=dsth, in_=srch).then_inc(*_dma(ctx, "qS"))
                s.dma_start(out=dstl, in_=srcl).then_inc(*_dma(ctx, "qS"))
            emit(nc, "sync", _)
        w2_spill_v = ctx.v("qS")

        def _(g):
            qn, qv = state["mm_spill"][0]
            g.wait_ge(ctx.s(qn), qv)
            g.collective_compute(
                "AllReduce", OP.add, replica_groups=[[0, 1], [2, 3], [4, 5], [6, 7]],
                ins=[io["bar_in"].opt()], outs=[io["bar_out"].opt()],
            ).then_inc(*_one(ctx, "CC"))
        emit(nc, "gpsimd", _)
        b2 = ctx.v("CC")

        kq_spill = []
        vt_spill = []
        x4_spill = []
        for i in range(4):
            m = DM >> i
            kcn = m // 128
            colq = slice(i * 512, (i + 1) * 512)

            def _(s, i=i, b2=b2):
                s.wait_ge(ctx.s("CC"), b2)
                if i >= 1:
                    s.wait_ge(ctx.s("eV"), blend_done)
                src00 = io["sh_mm"][0, 0, i, :, :].rearrange("(f p) c -> p f c", p=128)
                src10 = io["sh_mm"][1, 0, i, :, :].rearrange("(f p) c -> p f c", p=128)
                s.dma_start(out=bufB[:, :, :], in_=src00).then_inc(*_dma(ctx, "qS"))
                s.dma_start(out=bufA[:, :, :], in_=src10).then_inc(*_dma(ctx, "qS"))
            emit(nc, "sync", _)
            sv = ctx.v("qS")

            def _(g, i=i, sv=sv):
                g.wait_ge(ctx.s("qS"), sv)
                src01 = io["sh_mm"][0, 1, i, :, :].rearrange("(f p) c -> p f c", p=128)
                src11 = io["sh_mm"][1, 1, i, :, :].rearrange("(f p) c -> p f c", p=128)
                g.dma_start(out=bufB[:, :, :], in_=src01, accum_op=OP.add).then_inc(*_dma(ctx, "qG"))
                g.dma_start(out=bufA[:, :, :], in_=src11, accum_op=OP.add).then_inc(*_dma(ctx, "qG"))
            emit(nc, "gpsimd", _)
            gv = ctx.v("qG")

            def _(v, gv=gv, stat_0=stat_0):
                v.wait_ge(ctx.s("qG"), gv)
                v.wait_ge(ctx.s("eV"), stat_0)
                v.tensor_scalar_mul(bufB[:, :, :], bufB[:, :, :], mf[:, 0:1])
                v.scalar_tensor_tensor(bufB[:, :, :], bufA[:, :, :], mf[:, 1:2], bufB[:, :, :],
                                       op0=OP.mult, op1=OP.add).then_inc(*_one(ctx, "eV"))
            emit(nc, "vector", _)
            mm_comb = ctx.v("eV")

            tr_ev = {}
            for tc in range(4):
                def _(s, i=i, tc=tc):
                    if i >= 1 and tc == 0:
                        s.wait_ge(ctx.s("eV"), t4_done)
                    s.dma_start(out=xtile[:, :],
                                in_=io["x_own"][(i * 4 + tc) * 128:(i * 4 + tc + 1) * 128, :]).then_inc(*_dma(ctx, "qS"))
                emit(nc, "sync", _)
                xv = ctx.v("qS")

                def _(t, tc=tc, xv=xv, i=i):
                    t.wait_ge(ctx.s("qS"), xv)
                    if tr_ev.get(tc - 2) is not None:
                        t.wait_ge(ctx.s("eV"), tr_ev[tc - 2])
                    ba, bb = ps5[(2 * tc) % 4], ps5[(2 * tc + 1) % 4]
                    last = None
                    for dc in range(8):
                        dst = (ba if dc < 4 else bb)[:, (dc % 4) * 128:(dc % 4 + 1) * 128]
                        last = t.transpose(dst, xtile[:, dc * 128:(dc + 1) * 128], identf[:, :])
                    last.then_inc(*_one(ctx, "eT"))
                emit(nc, "tensor", _)
                tv = ctx.v("eT")

                def _(v, tc=tc, tv=tv):
                    v.wait_ge(ctx.s("eT"), tv)
                    ba, bb = ps5[(2 * tc) % 4], ps5[(2 * tc + 1) % 4]
                    dst1 = bass.AP(bufC, tc * 128, [[G * 512, 128], [512, 4], [1, 128]])
                    dst2 = bass.AP(bufC, 4 * 512 + tc * 128, [[G * 512, 128], [512, 4], [1, 128]])
                    src1 = bass.AP(ba, 0, [[512, 128], [128, 4], [1, 128]])
                    src2 = bass.AP(bb, 0, [[512, 128], [128, 4], [1, 128]])
                    v.tensor_copy(dst1, src1)
                    v.tensor_copy(dst2, src2).then_inc(*_one(ctx, "eV"))
                emit(nc, "vector", _)
                tr_ev[tc] = ctx.v("eV")
            xT_done = ctx.v("eV")

            def _(v, mm_comb=mm_comb):
                v.tensor_tensor(bufB[:, :, :], bufB[:, :, :], bufC[:, :, :], op=OP.add).then_inc(*_one(ctx, "eV"))
            emit(nc, "vector", _)
            x2_done = ctx.v("eV")

            sq_ev = {}
            for fc in range(G):
                def _(sc, fc=fc, x2_done=x2_done):
                    sc.wait_ge(ctx.s("eV"), x2_done)
                    tgt = t32a if fc % 2 == 0 else t32b
                    if sq_ev.get(fc - 2) is not None:
                        sc.wait_ge(ctx.s("eT"), sq_ev[fc - 2])
                    sc.activation(tgt[:, :], bufB[:, fc, :], AF.Square).then_inc(*_one(ctx, "eS"))
                emit(nc, "scalar", _)
                scv = ctx.v("eS")

                def _(t, fc=fc, scv=scv, i=i):
                    t.wait_ge(ctx.s("eS"), scv)
                    if fc == 0 and i >= 1:
                        t.wait_ge(ctx.s("eV"), rms_row_read)
                    tgt = t32a if fc % 2 == 0 else t32b
                    t.matmul(ps5[4][0:1, :], cb[:, 2:3], tgt[:, :],
                             start=(fc == 0), stop=(fc == G - 1)).then_inc(*_one(ctx, "eT"))
                emit(nc, "tensor", _)
                sq_ev[fc] = ctx.v("eT")
            ssq_done = ctx.v("eT")

            def _(v, ssq_done=ssq_done):
                v.wait_ge(ctx.s("eT"), ssq_done)
                v.tensor_copy(rowt[0:1, 0, :], ps5[4][0:1, :]).then_inc(*_one(ctx, "eV"))
            emit(nc, "vector", _)
            rms_row_read = ctx.v("eV")

            def _(sc, rms_row_read=rms_row_read):
                sc.wait_ge(ctx.s("eV"), rms_row_read)
                sc.activation(rowt[0:1, 1, :], rowt[0:1, 0, :], AF.Ln, bias=cb[0:1, 0:1], scale=1.0 / 1024.0)
                sc.activation(rowt[0:1, 1, :], rowt[0:1, 1, :], AF.Exp, scale=-0.5).then_inc(*_one(ctx, "eS"))
            emit(nc, "scalar", _)
            rsv = ctx.v("eS")

            def _(v, rsv=rsv):
                v.wait_ge(ctx.s("eS"), rsv)
                v.tensor_scalar(rowt[0:1, 2, :], rowt[0:1, 0, :], 1.0 / 1024.0, cb[0:1, 0:1], op0=OP.mult, op1=OP.add)
                v.tensor_tensor(rowt[0:1, 3, :], rowt[0:1, 1, :], rowt[0:1, 1, :], op=OP.mult)
                v.tensor_tensor(rowt[0:1, 3, :], rowt[0:1, 3, :], rowt[0:1, 2, :], op=OP.mult)
                v.tensor_scalar(rowt[0:1, 3, :], rowt[0:1, 3, :], -0.5, 1.5, op0=OP.mult, op1=OP.add)
                v.tensor_tensor(rowt[0:1, 1, :], rowt[0:1, 1, :], rowt[0:1, 3, :], op=OP.mult).then_inc(*_one(ctx, "eV"))
            emit(nc, "vector", _)
            rsn = ctx.v("eV")

            def _(s, rsn=rsn):
                s.wait_ge(ctx.s("eV"), rsn)
                s.dma_start(out=bass.AP(io["sp_rep"].tensor, 0, [[1, 1], [1, 512]]),
                            in_=rowt[0:1, 1, :]).then_inc(*_dma(ctx, "qS"))
                s.wait_ge(ctx.s("qS"), ctx.v("qS"))
                s.dma_start(out=rep1[:, :],
                            in_=bass.AP(io["sp_rep"].tensor, 0, [[0, 128], [1, 512]])).then_inc(*_dma(ctx, "qS"))
            emit(nc, "sync", _)
            repv = ctx.v("qS")

            def _(v, repv=repv):
                v.wait_ge(ctx.s("qS"), repv)
                bc = bass.AP(rep1, 0, [[512, 128], [0, G], [1, 512]])
                v.tensor_tensor(bufA[:, :, :], bufB[:, :, :], bc, op=OP.mult)
                v.tensor_copy(r2h[:, :, :], bufA[:, :, :])
                v.tensor_tensor(r2l[:, :, :], bufA[:, :, :], r2h[:, :, :], op=OP.subtract).then_inc(*_one(ctx, "eV"))
            emit(nc, "vector", _)
            r2_done = ctx.v("eV")

            w1_split = {}
            h1_done = {}
            for of in range(32):
                par = of % 2
                def _(s, of=of, i=i):
                    if w1_split.get(of - 1) is not None:
                        s.wait_ge(ctx.s("eV"), w1_split[of - 1])
                    src = bass.AP(io["w_m1_t"].tensor, of * 128,
                                  [[4 * DM, 128], [128 * 4 * DM, G], [1, 128]])
                    s.dma_start(out=wst[:, :, :], in_=src).then_inc(*_dma(ctx, "qS"))
                emit(nc, "sync", _)
                sv = ctx.v("qS")

                def _(g, of=of, par=par):
                    if h1_done.get(of - 2) is not None:
                        g.wait_ge(ctx.s("eT"), h1_done[of - 2][0])
                    src = bass.AP(io["w_m1_t"].tensor, of * 128,
                                  [[4 * DM, 128], [128 * 4 * DM, G], [1, 128]])
                    g.dma_start(out=wh[:, par, :, :], in_=src).then_inc(*_dma(ctx, "qG"))
                emit(nc, "gpsimd", _)
                gv = ctx.v("qG")

                def _(v, of=of, par=par, sv=sv, gv=gv, r2_done=r2_done):
                    v.wait_ge(ctx.s("qS"), sv)
                    v.wait_ge(ctx.s("qG"), gv)
                    if of == 0:
                        v.wait_ge(ctx.s("eV"), r2_done)
                    v.tensor_tensor(wl[:, par, :, :], wst[:, :, :], wh[:, par, :, :],
                                    op=OP.subtract).then_inc(*_one(ctx, "eV"))
                emit(nc, "vector", _)
                w1_split[of] = ctx.v("eV")

                def _(t, of=of, par=par, wv_=w1_split[of], i=i):
                    t.wait_ge(ctx.s("eV"), wv_)
                    if h1_done.get(of - 2) is not None:
                        t.wait_ge(ctx.s("eS"), h1_done[of - 2][1])
                    last = None
                    for k in range(G):
                        last = mm3(t, ps5[par][:, :], wh[:, par, k, :], wl[:, par, k, :],
                                   r2h[:, k, :], r2l[:, k, :], start=(k == 0), stop=(k == G - 1))
                    last.then_inc(*_one(ctx, "eT"))
                emit(nc, "tensor", _)
                mmv = ctx.v("eT")

                def _(sc, of=of, par=par, mmv=mmv):
                    sc.wait_ge(ctx.s("eT"), mmv)
                    if h1_done.get(of - 2) is not None:
                        sc.wait_ge(ctx.s("qS"), h1_done[of - 2][2])
                    tgt = t32a if par == 0 else t32b
                    sc.activation(tgt[:, :], ps5[par][:, :], AF.Silu).then_inc(*_one(ctx, "eS"))
                emit(nc, "scalar", _)
                siv = ctx.v("eS")

                def _(v, of=of, par=par, siv=siv):
                    v.wait_ge(ctx.s("eS"), siv)
                    tgt = t32a if par == 0 else t32b
                    v.tensor_copy(h1h[:, par, :], tgt[:, :])
                    v.tensor_tensor(h1l[:, par, :], tgt[:, :], h1h[:, par, :], op=OP.subtract).then_inc(*_one(ctx, "eV"))
                emit(nc, "vector", _)
                h1v = ctx.v("eV")

                def _(s, of=of, par=par, i=i, h1v=h1v):
                    s.wait_ge(ctx.s("eV"), h1v)
                    dsth = bass.AP(io["sp_h1h"].tensor, of * 128 * TO + i * 512, [[TO, 128], [1, 512]])
                    dstl = bass.AP(io["sp_h1l"].tensor, of * 128 * TO + i * 512, [[TO, 128], [1, 512]])
                    s.dma_start(out=dsth, in_=h1h[:, par, :]).then_inc(*_dma(ctx, "qS"))
                    s.dma_start(out=dstl, in_=h1l[:, par, :]).then_inc(*_dma(ctx, "qS"))
                emit(nc, "sync", _)
                h1_done[of] = (ctx.v("eT"), ctx.v("eS"), ctx.v("qS"), ctx.v("eV"))
            h1_all = ctx.v("qS")

            mm2_last = None
            h1ld = {}
            for of in range(32):
                par = of % 2
                def _(s, of=of, par=par, i=i, h1_all=h1_all, w2_spill_v=w2_spill_v):
                    if of < 2:
                        s.wait_ge(ctx.s("qS"), h1_all)
                        s.wait_ge(ctx.s("qS"), w2_spill_v)
                    if h1ld.get(of - 2) is not None:
                        s.wait_ge(ctx.s("eT"), h1ld[of - 2])
                    srch = bass.AP(io["sp_h1h"].tensor, of * 128 * TO + i * 512, [[TO, 128], [1, 512]])
                    srcl = bass.AP(io["sp_h1l"].tensor, of * 128 * TO + i * 512, [[TO, 128], [1, 512]])
                    s.dma_start(out=h1h[:, par, :], in_=srch).then_inc(*_dma(ctx, "qS"))
                    s.dma_start(out=h1l[:, par, :], in_=srcl).then_inc(*_dma(ctx, "qS"))
                    wsrch = bass.AP(io["sp_w2h"].tensor, of * 128 * DM, [[DM, 128], [128, G], [1, 128]])
                    wsrcl = bass.AP(io["sp_w2l"].tensor, of * 128 * DM, [[DM, 128], [128, G], [1, 128]])
                    s.dma_start(out=wh[:, par, :, :], in_=wsrch).then_inc(*_dma(ctx, "qS"))
                    s.dma_start(out=wl[:, par, :, :], in_=wsrcl).then_inc(*_dma(ctx, "qS"))
                emit(nc, "sync", _)
                sv = ctx.v("qS")

                def _(t, of=of, par=par, sv=sv):
                    t.wait_ge(ctx.s("qS"), sv)
                    if of == 0:
                        t.wait_ge(ctx.s("eV"), x2_done)
                    last = None
                    for f in range(G):
                        last = mm3(t, ps5[f][:, :], wh[:, par, f, :], wl[:, par, f, :],
                                   h1h[:, par, :], h1l[:, par, :],
                                   start=(of == 0), stop=(of == 31))
                    last.then_inc(*_one(ctx, "eT"))
                emit(nc, "tensor", _)
                h1ld[of] = ctx.v("eT")
            mlp2_done = ctx.v("eT")

            def _(v, mlp2_done=mlp2_done):
                v.wait_ge(ctx.s("eT"), mlp2_done)
                for f in range(G):
                    v.tensor_tensor(bufB[:, f, :], bufB[:, f, :], ps5[f][:, :], op=OP.add)
                v.tensor_tensor(bufB[:, :, :], bufB[:, :, :], bufC[:, :, :], op=OP.add).then_inc(*_one(ctx, "eV"))
            emit(nc, "vector", _)
            t4_done = ctx.v("eV")

            ln_ev = {}
            for fc in range(G):
                def _(sc, fc=fc, t4_done=t4_done):
                    sc.wait_ge(ctx.s("eV"), t4_done)
                    if ln_ev.get(fc - 2) is not None:
                        sc.wait_ge(ctx.s("eT"), ln_ev[fc - 2])
                    tgt = t32a if fc % 2 == 0 else t32b
                    sc.activation(tgt[:, :], bufB[:, fc, :], AF.Square).then_inc(*_one(ctx, "eS"))
                emit(nc, "scalar", _)
                scv = ctx.v("eS")

                def _(t, fc=fc, scv=scv, t4_done=t4_done):
                    t.wait_ge(ctx.s("eS"), scv)
                    tgt = t32a if fc % 2 == 0 else t32b
                    t.matmul(ps5[0][0:1, :], cb[:, 2:3], bufB[:, fc, :],
                             start=(fc == 0), stop=(fc == G - 1))
                    t.matmul(ps5[1][0:1, :], cb[:, 2:3], tgt[:, :],
                             start=(fc == 0), stop=(fc == G - 1)).then_inc(*_one(ctx, "eT"))
                emit(nc, "tensor", _)
                ln_ev[fc] = ctx.v("eT")
            ln_done = ctx.v("eT")

            def _(v, ln_done=ln_done):
                v.wait_ge(ctx.s("eT"), ln_done)
                v.tensor_scalar_mul(rowt[0:1, 0, :], ps5[0][0:1, :], 1.0 / 1024.0)
                v.tensor_scalar_mul(rowt[0:1, 1, :], ps5[1][0:1, :], 1.0 / 1024.0)
                v.tensor_tensor(rowt[0:1, 2, :], rowt[0:1, 0, :], rowt[0:1, 0, :], op=OP.mult)
                v.tensor_tensor(rowt[0:1, 2, :], rowt[0:1, 1, :], rowt[0:1, 2, :], op=OP.subtract).then_inc(*_one(ctx, "eV"))
            emit(nc, "vector", _)
            var_done = ctx.v("eV")

            def _(sc, var_done=var_done):
                sc.wait_ge(ctx.s("eV"), var_done)
                sc.activation(rowt[0:1, 3, :], rowt[0:1, 2, :], AF.Ln, bias=cb[0:1, 1:2])
                sc.activation(rowt[0:1, 3, :], rowt[0:1, 3, :], AF.Exp, scale=-0.5).then_inc(*_one(ctx, "eS"))
            emit(nc, "scalar", _)
            rsv2 = ctx.v("eS")

            def _(v, rsv2=rsv2):
                v.wait_ge(ctx.s("eS"), rsv2)
                v.tensor_scalar(rowt[0:1, 4, :], rowt[0:1, 2, :], 1.0, cb[0:1, 1:2], op0=OP.mult, op1=OP.add)
                v.tensor_tensor(rowt[0:1, 5, :], rowt[0:1, 3, :], rowt[0:1, 3, :], op=OP.mult)
                v.tensor_tensor(rowt[0:1, 5, :], rowt[0:1, 5, :], rowt[0:1, 4, :], op=OP.mult)
                v.tensor_scalar(rowt[0:1, 5, :], rowt[0:1, 5, :], -0.5, 1.5, op0=OP.mult, op1=OP.add)
                v.tensor_tensor(rowt[0:1, 3, :], rowt[0:1, 3, :], rowt[0:1, 5, :], op=OP.mult).then_inc(*_one(ctx, "eV"))
            emit(nc, "vector", _)
            rsn2 = ctx.v("eV")

            def _(s, rsn2=rsn2):
                s.wait_ge(ctx.s("eV"), rsn2)
                s.dma_start(out=bass.AP(io["sp_rep"].tensor, 512, [[1, 1], [1, 512]]),
                            in_=rowt[0:1, 0, :]).then_inc(*_dma(ctx, "qS"))
                s.dma_start(out=bass.AP(io["sp_rep"].tensor, 1024, [[1, 1], [1, 512]]),
                            in_=rowt[0:1, 3, :]).then_inc(*_dma(ctx, "qS"))
                s.wait_ge(ctx.s("qS"), ctx.v("qS"))
                s.dma_start(out=rep1[:, :],
                            in_=bass.AP(io["sp_rep"].tensor, 512, [[0, 128], [1, 512]])).then_inc(*_dma(ctx, "qS"))
                s.dma_start(out=rep2[:, :],
                            in_=bass.AP(io["sp_rep"].tensor, 1024, [[0, 128], [1, 512]])).then_inc(*_dma(ctx, "qS"))
            emit(nc, "sync", _)
            repv2 = ctx.v("qS")

            def _(v, repv2=repv2):
                v.wait_ge(ctx.s("qS"), repv2)
                bc1 = bass.AP(rep1, 0, [[512, 128], [0, G], [1, 512]])
                bc2 = bass.AP(rep2, 0, [[512, 128], [0, G], [1, 512]])
                v.tensor_tensor(bufB[:, :, :], bufB[:, :, :], bc1, op=OP.subtract)
                v.tensor_tensor(bufB[:, :, :], bufB[:, :, :], bc2, op=OP.mult).then_inc(*_one(ctx, "eV"))
            emit(nc, "vector", _)
            x4_done = ctx.v("eV")

            def _(s, i=i, x4_done=x4_done):
                s.wait_ge(ctx.s("eV"), x4_done)
                dst = bass.AP(io["sp_x4"].tensor, i * 512, [[TO, 128], [128 * TO, G], [1, 512]])
                s.dma_start(out=dst, in_=bufB[:, :, :]).then_inc(*_dma(ctx, "qS"))
            emit(nc, "sync", _)
            x4sp = ctx.v("qS")
            x4_spill.append(x4sp)

            def _(v, x4sp=x4sp):
                v.wait_ge(ctx.s("qS"), x4sp)
                for fc in range(G):
                    v.tensor_scalar(bufA[:, fc, :], bufB[:, fc, :], consts["norm_w"][:, fc:fc + 1],
                                    consts["norm_b"][:, fc:fc + 1], op0=OP.mult, op1=OP.add)
                v.tensor_copy(xnh[:, :, :], bufA[:, :, :])
                v.tensor_tensor(xnl[:, :, :], bufA[:, :, :], xnh[:, :, :], op=OP.subtract).then_inc(*_one(ctx, "eV"))
            emit(nc, "vector", _)
            xn_done = ctx.v("eV")

            stat_prev = {}
            for (wname, bname, statoff, spill_hi, spill_lo) in [
                ("w_k_t", "b_k", 8, "sh_k_hi", "sh_k_lo"),
                ("w_q_t", "b_q", 0, "sp_q_hi", "sp_q_lo"),
            ]:
                kq_ev = {}
                for f in range(G):
                    par = f % 2
                    def _(s, f=f, wname=wname, m=m, kcn=kcn):
                        if kq_ev.get(f - 1) is not None:
                            s.wait_ge(ctx.s("eV"), kq_ev[f - 1][0])
                        src = bass.AP(io[wname].tensor, f * 128,
                                      [[DM, 128], [128 * DM, kcn], [1, 128]])
                        s.dma_start(out=wst[:, 0:kcn, :], in_=src).then_inc(*_dma(ctx, "qS"))
                    emit(nc, "sync", _)
                    sv = ctx.v("qS")

                    def _(g, f=f, par=par, wname=wname, kcn=kcn):
                        if kq_ev.get(f - 2) is not None:
                            g.wait_ge(ctx.s("eT"), kq_ev[f - 2][1])
                        src = bass.AP(io[wname].tensor, f * 128,
                                      [[DM, 128], [128 * DM, kcn], [1, 128]])
                        g.dma_start(out=wh[:, par, 0:kcn, :], in_=src).then_inc(*_dma(ctx, "qG"))
                    emit(nc, "gpsimd", _)
                    gv = ctx.v("qG")

                    def _(v, f=f, par=par, sv=sv, gv=gv, kcn=kcn, xn_done=xn_done):
                        v.wait_ge(ctx.s("qS"), sv)
                        v.wait_ge(ctx.s("qG"), gv)
                        if f == 0:
                            v.wait_ge(ctx.s("eV"), xn_done)
                        v.tensor_tensor(wl[:, par, 0:kcn, :], wst[:, 0:kcn, :], wh[:, par, 0:kcn, :],
                                        op=OP.subtract).then_inc(*_one(ctx, "eV"))
                    emit(nc, "vector", _)
                    wsp = ctx.v("eV")

                    def _(t, f=f, par=par, wsp=wsp, kcn=kcn):
                        t.wait_ge(ctx.s("eV"), wsp)
                        if kq_ev.get(f - 2) is not None:
                            t.wait_ge(ctx.s("eS"), kq_ev[f - 2][2])
                        last = None
                        for k in range(kcn):
                            last = mm3(t, ps5[4 + par][:, :], wh[:, par, k, :], wl[:, par, k, :],
                                       xnh[:, k, :], xnl[:, k, :], start=(k == 0), stop=(k == kcn - 1))
                        last.then_inc(*_one(ctx, "eT"))
                    emit(nc, "tensor", _)
                    mmv = ctx.v("eT")

                    def _(v, f=f, par=par, mmv=mmv, bname=bname, shared=spill_hi.startswith("sh_"), i=i):
                        v.wait_ge(ctx.s("eT"), mmv)
                        if kq_ev.get(f - 2) is not None:
                            qn, qv = kq_ev[f - 2][3]
                            if qv:
                                v.wait_ge(ctx.s(qn), qv)
                        if shared and f == 0 and i >= 1:
                            qn, qv = kq_spill[-2]
                            if qv:
                                v.wait_ge(ctx.s(qn), qv)
                        if stat_prev.get("sq" + str(par)) is not None:
                            v.wait_ge(ctx.s("eS"), stat_prev["sq" + str(par)])
                        tgt = t32a if par == 0 else t32b
                        v.tensor_scalar_add(tgt[:, :], ps5[4 + par][:, :], consts[bname][:, f:f + 1])
                        hdst = kst_h[:, f, :] if shared else h1h[:, par, :]
                        ldst = kst_l[:, f, :] if shared else h1l[:, par, :]
                        v.tensor_copy(hdst, tgt[:, :])
                        v.tensor_tensor(ldst, tgt[:, :], hdst, op=OP.subtract).then_inc(*_one(ctx, "eV"))
                    emit(nc, "vector", _)
                    prj = ctx.v("eV")

                    def _(sc, f=f, par=par, prj=prj):
                        sc.wait_ge(ctx.s("eV"), prj)
                        if stat_prev.get("mm" + str(par)) is not None:
                            sc.wait_ge(ctx.s("eT"), stat_prev["mm" + str(par)])
                        tgt = t32a if par == 0 else t32b
                        sc.activation(sqb[:, par, :], tgt[:, :], AF.Square).then_inc(*_one(ctx, "eS"))
                    emit(nc, "scalar", _)
                    sqv = ctx.v("eS")
                    stat_prev["sq" + str(par)] = sqv

                    def _(t, f=f, par=par, sqv=sqv):
                        t.wait_ge(ctx.s("eS"), sqv)
                        if stat_prev.get("v") is not None:
                            t.wait_ge(ctx.s("eV"), stat_prev["v"])
                        t.matmul(ps5[6][0:1, :], cb[:, 2:3], sqb[:, par, :], start=True, stop=True).then_inc(*_one(ctx, "eT"))
                    emit(nc, "tensor", _)
                    nmv = ctx.v("eT")
                    stat_prev["mm" + str(par)] = nmv

                    def _(v, f=f, nmv=nmv, statoff=statoff, i=i):
                        v.wait_ge(ctx.s("eT"), nmv)
                        col = statoff * 4 + f * 4 + i
                        v.reduce_max(stat[0:1, col:col + 1], ps5[6][0:1, :],
                                     axis=mybir.AxisListType.X).then_inc(*_one(ctx, "eV"))
                    emit(nc, "vector", _)
                    statv = ctx.v("eV")
                    stat_prev["v"] = statv

                    if not spill_hi.startswith("sh_"):
                        def _(s, f=f, par=par, i=i, prj=prj, spill_hi=spill_hi, spill_lo=spill_lo):
                            s.wait_ge(ctx.s("eV"), prj)
                            dsth = bass.AP(io[spill_hi].tensor, f * 128 * TO + i * 512, [[TO, 128], [1, 512]])
                            dstl = bass.AP(io[spill_lo].tensor, f * 128 * TO + i * 512, [[TO, 128], [1, 512]])
                            s.dma_start(out=dsth, in_=h1h[:, par, :]).then_inc(*_dma(ctx, "qS"))
                            s.dma_start(out=dstl, in_=h1l[:, par, :]).then_inc(*_dma(ctx, "qS"))
                        emit(nc, "sync", _)
                        kq_ev[f] = (ctx.v("eV"), ctx.v("eT"), ctx.v("eS"), ("qS", ctx.v("qS")))
                    else:
                        kq_ev[f] = (ctx.v("eV"), ctx.v("eT"), ctx.v("eS"), ("qS", 0))
                if spill_hi.startswith("sh_"):
                    def _(s, i=i, done=kq_ev[G - 1][0]):
                        s.wait_ge(ctx.s("eV"), done)
                        dsth = bass.AP(io["sp_ksth"].tensor, i * 512, [[TO, 128], [128 * TO, G], [1, 512]])
                        dstl = bass.AP(io["sp_kstl"].tensor, i * 512, [[TO, 128], [128 * TO, G], [1, 512]])
                        s.dma_start(out=dsth, in_=kst_h[:, :, :]).then_inc(*_dma(ctx, "qS"))
                        s.dma_start(out=dstl, in_=kst_l[:, :, :]).then_inc(*_dma(ctx, "qS"))
                    emit(nc, "sync", _)
                    kq_spill.append(("qS", ctx.v("qS")))
                else:
                    kq_spill.append(("qS", kq_ev[G - 1][3][1]))

            nhv = (kcn + 3) // 4
            for hv in range(nhv):
                c0, c1 = hv * 4, min(kcn, hv * 4 + 4)
                nch = c1 - c0
                def _(s, c0=c0, nch=nch, hv=hv):
                    if hv == 0:
                        s.wait_ge(ctx.s("eV"), kq_ev[G - 1][0])
                    else:
                        s.wait_ge(ctx.s("eV"), wv_ready)
                    src = bass.AP(io["w_v_t"].tensor, c0 * 128 * DM, [[DM, 128], [128 * DM, nch], [1, DM]])
                    dst = bass.AP(bufC, 0, [[G * 512, 128], [DM, nch], [1, DM]])
                    s.dma_start(out=dst, in_=src).then_inc(*_dma(ctx, "qS"))
                emit(nc, "sync", _)
                sv = ctx.v("qS")

                def _(g, c0=c0, nch=nch):
                    src = bass.AP(io["w_v_t"].tensor, c0 * 128 * DM, [[DM, 128], [128 * DM, nch], [1, DM]])
                    dst = bass.AP(wvh, c0 * DM, [[G * DM, 128], [DM, nch], [1, DM]])
                    g.dma_start(out=dst, in_=src).then_inc(*_dma(ctx, "qG"))
                emit(nc, "gpsimd", _)
                gv = ctx.v("qG")

                def _(v, c0=c0, nch=nch, sv=sv, gv=gv):
                    v.wait_ge(ctx.s("qS"), sv)
                    v.wait_ge(ctx.s("qG"), gv)
                    stg = bass.AP(bufC, 0, [[G * 512, 128], [DM, nch], [1, DM]])
                    srch = bass.AP(wvh, c0 * DM, [[G * DM, 128], [DM, nch], [1, DM]])
                    dstl = bass.AP(wvl, c0 * DM, [[G * DM, 128], [DM, nch], [1, DM]])
                    v.tensor_tensor(dstl, stg, srch, op=OP.subtract).then_inc(*_one(ctx, "eV"))
                emit(nc, "vector", _)
                wv_ready = ctx.v("eV")

            vt_ev = {}
            for tc in range(4):
                for half in range(2):
                    u = tc * 2 + half
                    par = u % 2
                    def _(t, tc=tc, half=half, par=par, wv_ready=wv_ready, kcn=kcn, u=u):
                        if u == 0:
                            t.wait_ge(ctx.s("eV"), wv_ready)
                        if vt_ev.get(u - 2) is not None:
                            t.wait_ge(ctx.s("eV"), vt_ev[u - 2][0])

                        last = None
                        for k in range(kcn):
                            last = mm3(t, ps5[par][:, :],
                                       xnh[:, k, tc * 128:(tc + 1) * 128], xnl[:, k, tc * 128:(tc + 1) * 128],
                                       wvh[:, k, half * 512:(half + 1) * 512], wvl[:, k, half * 512:(half + 1) * 512],
                                       start=(k == 0), stop=(k == kcn - 1))
                        last.then_inc(*_one(ctx, "eT"))
                    emit(nc, "tensor", _)
                    mmv = ctx.v("eT")

                    def _(v, par=par, half=half, mmv=mmv, u=u, tc=tc, i=i):
                        v.wait_ge(ctx.s("eT"), mmv)
                        if u == 0 and i >= 1:
                            v.wait_ge(ctx.s("qS"), vt_spill[-1])
                        vsl = bass.AP(bufC, tc * DM + half * 512, [[G * 512, 128], [1, 512]])
                        v.tensor_tensor(vsl, ps5[par][:, :], bv_rep[:, half * 512:(half + 1) * 512],
                                        op=OP.add).then_inc(*_one(ctx, "eV"))
                    emit(nc, "vector", _)
                    vt_ev[u] = (ctx.v("eV"), 0)
            vt_all = ctx.v("eV")

            def _(s, i=i, vt_all=vt_all):
                s.wait_ge(ctx.s("eV"), vt_all)
                dst = bass.AP(io["sp_vst"].tensor, i * 512 * DM, [[DM, 128], [128 * DM, 4], [1, DM]])
                srcv = bass.AP(bufC, 0, [[G * 512, 128], [DM, 4], [1, DM]])
                s.dma_start(out=dst, in_=srcv).then_inc(*_dma(ctx, "qS"))
            emit(nc, "sync", _)
            vt_spill.append(ctx.v("qS"))
            blend_done = ctx.v("eV")

        def _(g):
            g.wait_ge(ctx.s("eV"), ctx.v("eV"))
            g.wait_ge(ctx.s("qS"), ctx.v("qS"))
            g.dma_start(out=io["sh_k_hi"][jreg, :, :], in_=io["sp_ksth"], bounds_check=None).then_inc(*_dma(ctx, "qG"))
            g.dma_start(out=io["sh_k_lo"][jreg, :, :], in_=io["sp_kstl"], bounds_check=None).then_inc(*_dma(ctx, "qG"))
            g.dma_start(out=io["sh_vt"][jreg, :, :], in_=io["sp_vst"], bounds_check=None).then_inc(*_dma(ctx, "qG"))
            g.dma_start(out=io["sh_st"][jreg, :, :], in_=stat[0:1, :], bounds_check=None).then_inc(*_dma(ctx, "qG"))
        emit(nc, "gpsimd", _)
        stw = ctx.v("qG")

        def _(g, stw=stw):
            g.wait_ge(ctx.s("qG"), stw)
            g.collective_compute(
                "AllReduce", OP.add, replica_groups=[[0, 1], [2, 3], [4, 5], [6, 7]],
                ins=[io["bar_in"].opt()], outs=[io["bar_out"].opt()],
            ).then_inc(*_one(ctx, "CC"))
        emit(nc, "gpsimd", _)
        b3 = ctx.v("CC")
        state["b3"] = b3

        def _(s, b3=b3):
            s.wait_ge(ctx.s("CC"), b3)
            s.dma_start(out=rowt[0:1, 0, 0:64], in_=io["sh_st"][0, :, :]).then_inc(*_dma(ctx, "qS"))
            s.dma_start(out=rowt[0:1, 1, 0:64], in_=io["sh_st"][1, :, :]).then_inc(*_dma(ctx, "qS"))
        emit(nc, "sync", _)
        strv = ctx.v("qS")

        def _(v, strv=strv):
            v.wait_ge(ctx.s("qS"), strv)
            v.tensor_scalar_mul(rowt[0:1, 2, 0:32], rowt[0:1, 0, 0:32], mf[0:1, 0:1])
            v.scalar_tensor_tensor(rowt[0:1, 3, 0:32], rowt[0:1, 1, 0:32], mf[0:1, 1:2], rowt[0:1, 2, 0:32],
                                   op0=OP.mult, op1=OP.add)
            v.tensor_max(rowt[0:1, 4, 0:32], rowt[0:1, 0, 32:64], rowt[0:1, 1, 32:64])
            def hview(row, off):
                return bass.AP(rowt, row * 512 + off, [[6 * 512, 1], [4, 8]])
            v.tensor_max(rowt[0:1, 2, 16:24], hview(3, 0), hview(3, 1))
            v.tensor_max(rowt[0:1, 2, 24:32], hview(3, 2), hview(3, 3))
            v.tensor_max(rowt[0:1, 2, 0:8], rowt[0:1, 2, 16:24], rowt[0:1, 2, 24:32])
            v.tensor_max(rowt[0:1, 2, 32:40], hview(4, 0), hview(4, 1))
            v.tensor_max(rowt[0:1, 2, 40:48], hview(4, 2), hview(4, 3))
            v.tensor_max(rowt[0:1, 2, 8:16], rowt[0:1, 2, 32:40], rowt[0:1, 2, 40:48]).then_inc(*_one(ctx, "eV"))
        emit(nc, "vector", _)
        mx = ctx.v("eV")

        def _(sc, mx=mx):
            sc.wait_ge(ctx.s("eV"), mx)
            sc.activation(rowt[0:1, 5, 0:16], rowt[0:1, 2, 0:16], AF.Sqrt).then_inc(*_one(ctx, "eS"))
        emit(nc, "scalar", _)
        sq2 = ctx.v("eS")

        def _(g, sq2=sq2):
            g.wait_ge(ctx.s("eS"), sq2)
            g.tensor_tensor(rowt[0:1, 4, 16:24], rowt[0:1, 5, 0:8], rowt[0:1, 5, 8:16], op=OP.mult)
            g.tensor_scalar_mul(rowt[0:1, 4, 0:8], rowt[0:1, 4, 16:24], -SC * 1.01).then_inc(*_one(ctx, "eG"))
        emit(nc, "gpsimd", _)
        mrow = ctx.v("eG")

        def _(s, mrow=mrow):
            s.wait_ge(ctx.s("eG"), mrow)
            s.dma_start(out=bass.AP(io["sp_rep"].tensor, 1536, [[1, 1], [1, 8]]),
                        in_=rowt[0:1, 4, 0:8]).then_inc(*_dma(ctx, "qS"))
            s.wait_ge(ctx.s("qS"), ctx.v("qS"))
            s.dma_start(out=mneg[:, :],
                        in_=bass.AP(io["sp_rep"].tensor, 1536, [[0, 128], [1, 8]])).then_inc(*_dma(ctx, "qS"))
        emit(nc, "sync", _)
        state["mneg_v"] = ctx.v("qS")
        state["kq_spill"] = kq_spill
        state["vt_spill"] = vt_spill
        state["x4_spill"] = x4_spill

        if "ph5" in debug:
            def _(s):
                s.wait_ge(ctx.s("qS"), ctx.v("qS"))
                s.dma_start(out=dbg["x4"], in_=io["sp_x4"]).then_inc(*_dma(ctx, "qS"))
                s.dma_start(out=dbg["mst"], in_=mneg[:, :]).then_inc(*_dma(ctx, "qS"))
                s.dma_start(out=dbg["stat"], in_=stat[0:1, :]).then_inc(*_dma(ctx, "qS"))
                s.dma_start(out=dbg["rowt"], in_=rowt[0:1, :, :]).then_inc(*_dma(ctx, "qS"))
            emit(nc, "sync", _)

            def _(s):
                s.wait_ge(ctx.s("qS"), ctx.v("qS"))
                s.wait_ge(ctx.s("qG"), ctx.v("qG"))
                s.dma_start(out=dbg["kqh"], in_=io["sp_ksth"]).then_inc(*_dma(ctx, "qS"))
                s.dma_start(out=dbg["kql"], in_=io["sp_kstl"]).then_inc(*_dma(ctx, "qS"))
                s.dma_start(out=dbg["vt"], in_=io["sp_vst"]).then_inc(*_dma(ctx, "qS"))
                s.wait_ge(ctx.s("qS"), ctx.v("qS"))
            emit(nc, "sync", _)
    return dbg


def build_ph67(ctx, state, debug):
    nc, io, consts = state["nc"], state["io"], state["consts"]
    cb = state["cb"]
    identf = state["ident"]
    mneg = state["mneg"]
    from contextlib import ExitStack
    dbg = {}
    if "ph6" in debug:
        dbg["attn"] = nc.dram_tensor("dbg_attn", [DM, TO], F32, kind="ExternalOutput").ap()
        io["sp_attn"] = nc.dram_tensor("sp_attn", [DM, TO], F32).ap()
        dbg["t5"] = nc.dram_tensor("dbg_t5", [DM, TO], F32, kind="ExternalOutput").ap()
        io["sp_t5"] = nc.dram_tensor("sp_t5", [DM, TO], F32).ap()
        dbg["t5T"] = nc.dram_tensor("dbg_t5T", [128, 1024], F32, kind="ExternalOutput").ap()
        dbg["row6"] = nc.dram_tensor("dbg_row6", [128, 12], F32, kind="ExternalOutput").ap()
    SC = 1.0 / 32.0

    with ExitStack() as ph:
        psb = lambda name, shape, dt: ph.enter_context(nc.sbuf_tensor(name, shape, dt))
        psp = lambda name, shape, dt: ph.enter_context(nc.psum_tensor(name, shape, dt))
        qh_t = psb("qh_t", [128, 512], BF16)
        ql_t = psb("ql_t", [128, 512], BF16)
        kh_t = psb("kh_t", [128, 2, TO], BF16)
        kl_t = psb("kl_t", [128, 2, TO], BF16)
        vt_sb = psb("vt_sb", [128, 32, 128], F32)
        p_sb = psb("p_sb", [128, 4, 512], F32)
        den = psb("den", [128, 2, 512], F32)
        attnT = psb("attnT", [128, G, 512], F32)
        ath = psb("ath", [128, G, 512], BF16)
        atl = psb("atl", [128, G, 512], BF16)
        x4q = psb("x4q", [128, G, 512], F32)
        t5 = psb("t5", [128, G, 512], F32)
        wst6 = psb("wst6", [128, G, 128], F32)
        wh6 = psb("wh6", [128, 2, G, 128], BF16)
        wl6 = psb("wl6", [128, 2, G, 128], BF16)
        t32c = psb("t32c", [128, 1024], F32)
        t32d = psb("t32d", [128, 1024], F32)
        rep6 = psb("rep6", [128, 512], F32)
        row6 = psb("row6", [128, 12], F32)
        trow = psb("trow", [1, 2, 512], F32)
        ps6a = psp("ps6a", [128, 1024], F32)
        ps6b = psp("ps6b", [128, 1024], F32)
        ps6o = psp("ps6o", [128, 512], F32)
        ps6d = psp("ps6d", [128, 512], F32)
        ps6t = psp("ps6t", [128, 1024], F32)

        b3 = state["b3"]
        prev = {}
        y_written = []
        for qb in range(4):
            m = DM >> qb
            kcn = m // 128

            def _(s, qb=qb, b3=b3):
                if qb == 0:
                    s.wait_ge(ctx.s("CC"), b3)
                if prev.get("x4_used") is not None:
                    s.wait_ge(ctx.s("eV"), prev["x4_used"])
                src = bass.AP(io["sp_x4"].tensor, qb * 512, [[TO, 128], [128 * TO, G], [1, 512]])
                s.dma_start(out=x4q[:, :, :], in_=src).then_inc(*_dma(ctx, "qS"))
            emit(nc, "sync", _)
            x4v = ctx.v("qS")

            for h in range(H):
                def _(s, qb=qb, h=h):
                    if prev.get("qk_used") is not None:
                        s.wait_ge(ctx.s("eT"), prev["qk_used"])
                    qsrch = bass.AP(io["sp_q_hi"].tensor, h * 128 * TO + qb * 512, [[TO, 128], [1, 512]])
                    qsrcl = bass.AP(io["sp_q_lo"].tensor, h * 128 * TO + qb * 512, [[TO, 128], [1, 512]])
                    s.dma_start(out=qh_t[:, :], in_=qsrch).then_inc(*_dma(ctx, "qS"))
                    s.dma_start(out=ql_t[:, :], in_=qsrcl).then_inc(*_dma(ctx, "qS"))
                    for sl in range(2):
                        kh_src = io["sh_k_hi"][sl, h * 128:(h + 1) * 128, :]
                        kl_src = io["sh_k_lo"][sl, h * 128:(h + 1) * 128, :]
                        s.dma_start(out=kh_t[:, sl, :], in_=kh_src).then_inc(*_dma(ctx, "qS"))
                        s.dma_start(out=kl_t[:, sl, :], in_=kl_src).then_inc(*_dma(ctx, "qS"))
                        vsrc = io["sh_vt"][sl, :, h * 128:(h + 1) * 128].rearrange("(kt p) c -> p kt c", p=128)
                        dst = vt_sb[:, sl * 16:(sl + 1) * 16, :]
                        s.dma_start(out=dst, in_=vsrc).then_inc(*_dma(ctx, "qS"))
                emit(nc, "sync", _)
                ldv = ctx.v("qS")

                exp_prev = {}
                pv_last = None
                for kt in range(32):
                    gi = (kt // 2) % 2
                    bank = ps6a if gi == 0 else ps6b
                    half = kt % 2
                    def _(t, kt=kt, h=h, gi=gi, half=half, bank=bank, ldv=ldv, qb=qb):
                        if kt == 0:
                            t.wait_ge(ctx.s("qS"), ldv)
                            if prev.get("psum6") is not None:
                                t.wait_ge(ctx.s("eV"), prev["psum6"])
                        if exp_prev.get(kt - 4) is not None:
                            t.wait_ge(ctx.s("eS"), exp_prev[kt - 4])
                        sl, lt = kt // 16, kt % 16
                        last = mm3(t, bank[:, half * 512:(half + 1) * 512],
                                   kh_t[:, sl, lt * 128:(lt + 1) * 128], kl_t[:, sl, lt * 128:(lt + 1) * 128],
                                   qh_t[:, :], ql_t[:, :], start=True, stop=True)
                        if kt == 0:
                            last.then_inc(*_one(ctx, "eT"))
                        elif half == 1:
                            last.then_inc(*_one(ctx, "eT"))
                        else:
                            last.then_inc(*_one(ctx, "eT"))
                    emit(nc, "tensor", _)
                    st_mm = ctx.v("eT")

                    if half == 1:
                        def _(sc, kt=kt, h=h, gi=gi, bank=bank, st_mm=st_mm):
                            sc.wait_ge(ctx.s("eT"), st_mm)
                            if exp_prev.get(kt - 2 - 1) is not None and prev.get("pv" + str(gi)) is not None:
                                sc.wait_ge(ctx.s("eT"), prev["pv" + str(gi)])
                            pi = (kt // 2) % 2
                            sc.activation(p_sb[:, 2 * pi:2 * pi + 2, :].rearrange("p a c -> p (a c)"),
                                          bank[:, :], AF.Exp, bias=mneg[:, h:h + 1],
                                          scale=SC).then_inc(*_one(ctx, "eS"))
                        emit(nc, "scalar", _)
                        exp_prev[kt] = ctx.v("eS")

                        for sub in range(2):
                            ktt = kt - 1 + sub
                            pi = (kt // 2) % 2
                            eng = "vector" if (ktt % 2 == 0) else "gpsimd"
                            def _(e, ktt=ktt, pi=pi, sub=sub, h=h, ev=exp_prev[kt]):
                                wsem = "eV" if (ktt % 2 == 0) else "eG"
                                e.wait_ge(ctx.s("eS"), ev)
                                dslice = den[:, 0, :] if (ktt % 2 == 0) else den[:, 1, :]
                                if ktt <= 1:
                                    if prev.get("dred") is not None:
                                        e.wait_ge(ctx.s("eT"), prev["dred"])
                                    e.tensor_copy(dslice, p_sb[:, 2 * pi + sub, :]).then_inc(*_one(ctx, wsem))
                                else:
                                    e.tensor_tensor(dslice, dslice, p_sb[:, 2 * pi + sub, :],
                                                    op=OP.add).then_inc(*_one(ctx, wsem))
                            emit(nc, eng, _)

                            def _(t, ktt=ktt, pi=pi, sub=sub, ev=exp_prev[kt], kt=kt):
                                t.wait_ge(ctx.s("eS"), ev)
                                t.matmul(ps6o[:, :], vt_sb[:, ktt, :], p_sb[:, 2 * pi + sub, :],
                                         start=(ktt == 0), stop=(ktt == 31)).then_inc(*_one(ctx, "eT"))
                            emit(nc, "tensor", _)
                            pv_last = ctx.v("eT")
                        prev["pv" + str((kt // 2) % 2)] = pv_last
                den_v = ctx.v("eV")
                den_g = ctx.v("eG")

                def _(v, den_v=den_v, den_g=den_g):
                    v.wait_ge(ctx.s("eG"), den_g)
                    v.tensor_tensor(den[:, 0, :], den[:, 0, :], den[:, 1, :], op=OP.add).then_inc(*_one(ctx, "eV"))
                emit(nc, "vector", _)
                dsum = ctx.v("eV")

                def _(t, dsum=dsum):
                    t.wait_ge(ctx.s("eV"), dsum)
                    t.matmul(ps6d[0:1, :], cb[:, 2:3], den[:, 0, :], start=True, stop=True).then_inc(*_one(ctx, "eT"))
                emit(nc, "tensor", _)
                dred = ctx.v("eT")
                prev["dred"] = dred

                def _(v, dred=dred):
                    v.wait_ge(ctx.s("eT"), dred)
                    v.reciprocal(trow[0:1, 0, :], ps6d[0:1, :]).then_inc(*_one(ctx, "eV"))
                emit(nc, "vector", _)
                drec = ctx.v("eV")

                def _(s, drec=drec, h=h, qb=qb):
                    s.wait_ge(ctx.s("eV"), drec)
                    s.dma_start(out=bass.AP(io["sp_rep"].tensor, 0, [[1, 1], [1, 512]]),
                                in_=trow[0:1, 0, :]).then_inc(*_dma(ctx, "qS"))
                    s.wait_ge(ctx.s("qS"), ctx.v("qS"))
                    s.dma_start(out=rep6[:, :],
                                in_=bass.AP(io["sp_rep"].tensor, 0, [[0, 128], [1, 512]])).then_inc(*_dma(ctx, "qS"))
                emit(nc, "sync", _)
                repv = ctx.v("qS")

                def _(v, repv=repv, h=h, pv_last=pv_last):
                    v.wait_ge(ctx.s("qS"), repv)
                    v.wait_ge(ctx.s("eT"), pv_last)
                    v.tensor_tensor(attnT[:, h, :], ps6o[:, :], rep6[:, :], op=OP.mult).then_inc(*_one(ctx, "eV"))
                emit(nc, "vector", _)
                prev["psum6"] = ctx.v("eV")
                prev["qk_used"] = ctx.v("eT")
            attn_done = ctx.v("eV")

            if "ph6" in debug:
                def _(s, qb=qb, attn_done=attn_done):
                    s.wait_ge(ctx.s("eV"), attn_done)
                    dst = bass.AP(io["sp_attn"].tensor, qb * 512, [[TO, 128], [128 * TO, G], [1, 512]])
                    s.dma_start(out=dst, in_=attnT[:, :, :]).then_inc(*_dma(ctx, "qS"))
                emit(nc, "sync", _)

            def _(v, attn_done=attn_done, kcn=kcn):
                v.tensor_copy(ath[:, 0:kcn, :], attnT[:, 0:kcn, :])
                v.tensor_tensor(atl[:, 0:kcn, :], attnT[:, 0:kcn, :], ath[:, 0:kcn, :],
                                op=OP.subtract).then_inc(*_one(ctx, "eV"))
            emit(nc, "vector", _)
            at_split = ctx.v("eV")

            op_ev = {}
            for fo in range(G):
                par = fo % 2
                if fo < kcn:
                    def _(s, fo=fo, kcn=kcn):
                        if op_ev.get(fo - 1) is not None:
                            s.wait_ge(ctx.s("eV"), op_ev[fo - 1][0])
                        src = bass.AP(io["w_o_t"].tensor, fo * 128, [[DM, 128], [128 * DM, kcn], [1, 128]])
                        s.dma_start(out=wst6[:, 0:kcn, :], in_=src).then_inc(*_dma(ctx, "qS"))
                    emit(nc, "sync", _)
                    sv = ctx.v("qS")

                    def _(g, fo=fo, par=par, kcn=kcn):
                        if op_ev.get(fo - 2) is not None:
                            g.wait_ge(ctx.s("eT"), op_ev[fo - 2][1])
                        src = bass.AP(io["w_o_t"].tensor, fo * 128, [[DM, 128], [128 * DM, kcn], [1, 128]])
                        g.dma_start(out=wh6[:, par, 0:kcn, :], in_=src).then_inc(*_dma(ctx, "qG"))
                    emit(nc, "gpsimd", _)
                    gv = ctx.v("qG")

                    def _(v, fo=fo, par=par, sv=sv, gv=gv, kcn=kcn, at_split=at_split):
                        v.wait_ge(ctx.s("qS"), sv)
                        v.wait_ge(ctx.s("qG"), gv)
                        if fo == 0:
                            v.wait_ge(ctx.s("eV"), at_split)
                        v.tensor_tensor(wl6[:, par, 0:kcn, :], wst6[:, 0:kcn, :], wh6[:, par, 0:kcn, :],
                                        op=OP.subtract).then_inc(*_one(ctx, "eV"))
                    emit(nc, "vector", _)
                    wspl = ctx.v("eV")

                    def _(t, fo=fo, par=par, wspl=wspl, kcn=kcn):
                        t.wait_ge(ctx.s("eV"), wspl)
                        last = None
                        for k in range(kcn):
                            last = mm3(t, ps6d[:, :] if par == 0 else ps6o[:, :],
                                       wh6[:, par, k, :], wl6[:, par, k, :],
                                       ath[:, k, :], atl[:, k, :], start=(k == 0), stop=(k == kcn - 1))
                        last.then_inc(*_one(ctx, "eT"))
                    emit(nc, "tensor", _)
                    mmv = ctx.v("eT")

                    def _(v, fo=fo, par=par, mmv=mmv, x4v=x4v):
                        v.wait_ge(ctx.s("eT"), mmv)
                        v.wait_ge(ctx.s("qS"), x4v)
                        ps = ps6d if par == 0 else ps6o
                        v.tensor_scalar_add(t5[:, fo, :], ps[:, :], consts["b_o"][:, fo:fo + 1])
                        v.scalar_tensor_tensor(t5[:, fo, :], x4q[:, fo, :], 2.0, t5[:, fo, :],
                                               op0=OP.mult, op1=OP.add).then_inc(*_one(ctx, "eV"))
                    emit(nc, "vector", _)
                    op_ev[fo] = (ctx.v("eV"), ctx.v("eT"))
                else:
                    def _(v, fo=fo, x4v=x4v):
                        v.wait_ge(ctx.s("qS"), x4v)
                        v.tensor_scalar_mul(t5[:, fo, :], x4q[:, fo, :], 2.0).then_inc(*_one(ctx, "eV"))
                    emit(nc, "vector", _)
                    op_ev[fo] = (ctx.v("eV"), ctx.v("eT"))
            t5_done = ctx.v("eV")
            prev["x4_used"] = t5_done

            if "ph6" in debug:
                def _(s, qb=qb, t5_done=t5_done):
                    s.wait_ge(ctx.s("eV"), t5_done)
                    dst = bass.AP(io["sp_t5"].tensor, qb * 512, [[TO, 128], [128 * TO, G], [1, 512]])
                    s.dma_start(out=dst, in_=t5[:, :, :]).then_inc(*_dma(ctx, "qS"))
                emit(nc, "sync", _)

            tr_prev = {}
            for tc in range(4):
                def _(t, tc=tc, t5_done=t5_done, qb=qb):
                    t.wait_ge(ctx.s("eV"), t5_done)
                    if tr_prev.get(tc - 1) is not None:
                        t.wait_ge(ctx.s("eV"), tr_prev[tc - 1])
                    last = None
                    for dc in range(8):
                        half = 0 if dc < 4 else 1
                        dst = ps6t[:, half * 512 + (dc % 4) * 128:half * 512 + (dc % 4 + 1) * 128]
                        last = t.transpose(dst, t5[:, dc, tc * 128:(tc + 1) * 128], identf[:, :])
                    last.then_inc(*_one(ctx, "eT"))
                emit(nc, "tensor", _)
                trv = ctx.v("eT")

                def _(v, tc=tc, trv=trv):
                    v.wait_ge(ctx.s("eT"), trv)
                    src = bass.AP(ps6t, 0, [[1024, 128], [128, 8], [1, 128]])
                    dst = bass.AP(t32c, 0, [[1024, 128], [128, 8], [1, 128]])
                    v.tensor_copy(dst, src).then_inc(*_one(ctx, "eV"))
                emit(nc, "vector", _)
                evv = ctx.v("eV")
                tr_prev[tc] = evv

                def _(sc, tc=tc, evv=evv):
                    sc.wait_ge(ctx.s("eV"), evv)
                    if y_written:
                        sc.wait_ge(ctx.s("qO"), y_written[-1])
                    sc.activation(t32d[:, :], t32c[:, :], AF.Copy, accum_out=row6[:, 0:1])
                    sc.activation(t32d[:, :], t32c[:, :], AF.Square, accum_out=row6[:, 1:2]).then_inc(*_one(ctx, "eS"))
                emit(nc, "scalar", _)
                sqv = ctx.v("eS")

                def _(g, sqv=sqv):
                    g.wait_ge(ctx.s("eS"), sqv)
                    g.tensor_scalar_mul(row6[:, 2:3], row6[:, 0:1], 1.0 / 1024.0)
                    g.tensor_scalar_mul(row6[:, 3:4], row6[:, 1:2], 1.0 / 1024.0)
                    g.tensor_tensor(row6[:, 4:5], row6[:, 2:3], row6[:, 2:3], op=OP.mult)
                    g.tensor_tensor(row6[:, 5:6], row6[:, 3:4], row6[:, 4:5], op=OP.subtract)
                    g.tensor_scalar(row6[:, 6:7], row6[:, 5:6], 1.0, cb[:, 1:2], op0=OP.mult, op1=OP.add).then_inc(*_one(ctx, "eG"))
                emit(nc, "gpsimd", _)
                vvar = ctx.v("eG")

                def _(sc, vvar=vvar):
                    sc.wait_ge(ctx.s("eG"), vvar)
                    sc.activation(row6[:, 7:8], row6[:, 5:6], AF.Ln, bias=cb[:, 1:2])
                    sc.activation(row6[:, 8:9], row6[:, 7:8], AF.Exp, scale=-0.5).then_inc(*_one(ctx, "eS"))
                emit(nc, "scalar", _)
                rsq = ctx.v("eS")

                def _(g, rsq=rsq, tc=tc):
                    g.wait_ge(ctx.s("eS"), rsq)
                    g.tensor_tensor(row6[:, 9:10], row6[:, 8:9], row6[:, 8:9], op=OP.mult)
                    g.tensor_tensor(row6[:, 10:11], row6[:, 9:10], row6[:, 6:7], op=OP.mult)
                    g.tensor_scalar(row6[:, 11:12], row6[:, 10:11], -0.5, 1.5, op0=OP.mult, op1=OP.add)
                    g.tensor_tensor(row6[:, 4:5], row6[:, 8:9], row6[:, 11:12], op=OP.mult).then_inc(*_one(ctx, "eG"))
                emit(nc, "gpsimd", _)
                nrv = ctx.v("eG")

                def _(v, nrv=nrv, tc=tc):
                    v.wait_ge(ctx.s("eG"), nrv)
                    v.tensor_scalar(t32d[:, :], t32c[:, :], row6[:, 2:3], row6[:, 4:5], op0=OP.subtract, op1=OP.mult).then_inc(*_one(ctx, "eV"))
                emit(nc, "vector", _)
                outv = ctx.v("eV")

                def _(s, qb=qb, tc=tc, outv=outv):
                    s.wait_ge(ctx.s("eV"), outv)
                    s.dma_start(out=io["y_out"][qb * 512 + tc * 128:qb * 512 + (tc + 1) * 128, :],
                                in_=t32d[:, :]).then_inc(*_dma(ctx, "qO"))
                    if "ph6" in debug and qb == 0 and tc == 0:
                        s.dma_start(out=dbg["t5T"], in_=t32c[:, :]).then_inc(*_dma(ctx, "qS"))
                        s.dma_start(out=dbg["row6"], in_=row6[:, :]).then_inc(*_dma(ctx, "qS"))
                emit(nc, "sync", _)
                y_written.append(ctx.v("qO"))

        def _(s):
            s.wait_ge(ctx.s("qO"), y_written[-1])
        emit(nc, "sync", _)

        if "ph6" in debug:
            def _(s):
                s.wait_ge(ctx.s("qS"), ctx.v("qS"))
                s.dma_start(out=dbg["attn"], in_=io["sp_attn"]).then_inc(*_dma(ctx, "qS"))
                s.dma_start(out=dbg["t5"], in_=io["sp_t5"]).then_inc(*_dma(ctx, "qS"))
                s.wait_ge(ctx.s("qS"), ctx.v("qS"))
            emit(nc, "sync", _)
    return dbg


from concourse.bass_utils import run_bass_kernel_spmd

_CACHE = {}


def _get_program():
    if "nc" not in _CACHE:
        _CACHE["nc"] = build(debug=set())
    return _CACHE["nc"]


def kernel(**inputs):
    nc, _state = _get_program()
    in_maps = [host_layout(inputs, c) for c in range(8)]
    res = run_bass_kernel_spmd(nc, in_maps, core_ids=list(range(8)))
    out = np.zeros((4, 4096, 1024), np.float32)
    for c in range(8):
        b, j = c // 2, c % 2
        y = np.asarray(res.results[c]["y_out"])
        for i in range(4):
            out[b, (2 * i + j) * 512:(2 * i + j + 1) * 512] = y[i * 512:(i + 1) * 512]
    return out
